# version 03
# baseline (speedup 1.0000x reference)
import contextlib
import numpy as np
import ml_dtypes
import concourse.bass as bass
import concourse.mybir as mybir
from concourse.bass_utils import run_bass_kernel_spmd

F32 = mybir.dt.float32
BF16 = mybir.dt.bfloat16
AF = mybir.ActivationFunctionType
ALU = mybir.AluOpType
AX = mybir.AxisListType

D = 2048
KC = 16
TOK = 2048
NCH = 16
HALF = 1024
NHC = 8
NH = 16
HPG = 2
NG = NH // HPG
NE = 18
ICH = 11
EPS = 1e-5

DBG = False

ENG_SEM = {"pe": "pe", "ve": "ve", "se": "se", "ge": "ge",
           "sydma": "dx", "gpdma": "dw"}
ENG_PHYS = {"pe": "tensor", "ve": "vector", "se": "scalar", "ge": "gpsimd",
            "sydma": "sync", "gpdma": "gpsimd"}


class Prog:
    def __init__(self, nc):
        self.nc = nc
        self.sems = {}
        self.cnt = {}
        self.items = []
        self.last_w = {}
        self.readers = {}
        self.waited = {}

    def init_sems(self, stack):
        for s in ["pe", "ve", "se", "ge"]:
            self.sems[s] = stack.enter_context(self.nc.semaphore(s))
            self.cnt[s] = 0
        for q in ["dx", "dw"]:
            for i in range(8):
                s = f"{q}{i}"
                self.sems[s] = stack.enter_context(self.nc.semaphore(s))
                self.cnt[s] = 0
        self.dma_rr = {"dx": 0, "dw": 0}

    def op(self, eng, fn, rd=(), wr=(), deps=()):
        sem = ENG_SEM[eng]
        phys = ENG_PHYS[eng]
        if sem in ("dx", "dw"):
            inc = 16
            i = self.dma_rr[sem]
            self.dma_rr[sem] = (i + 1) % 8
            sem = f"{sem}{i}"
        else:
            inc = 1
        need = {}

        def add(d):
            if d is None:
                return
            s, t = d
            if need.get(s, -1) < t:
                need[s] = t

        for r in rd:
            add(self.last_w.get(r))
        for r in wr:
            add(self.last_w.get(r))
            for s, t in self.readers.get(r, {}).items():
                add((s, t))
        for d in deps:
            add(d)
        w = self.waited.setdefault(phys, {})
        waits = {}
        for s, t in sorted(need.items()):
            if w.get(s, -1) >= t:
                continue
            waits[s] = t
            w[s] = t
        self.cnt[sem] += inc
        tgt = self.cnt[sem]
        label = (sem, tgt)
        for r in rd:
            self.readers.setdefault(r, {})
            if self.readers[r].get(sem, -1) < tgt:
                self.readers[r][sem] = tgt
        for r in wr:
            self.last_w[r] = label
            self.readers[r] = {}
        self.items.append((phys, fn, waits, sem, inc))
        return label

    def flush(self, name):
        if not self.items:
            return
        items = self.items
        self.items = []
        by_eng = {}
        for it in items:
            by_eng.setdefault(it[0], []).append(it)

        def mk_body(eng_items):
            def body(eng):
                for _, fn, waits, sem, inc in eng_items:
                    for s, t in waits.items():
                        eng.wait_ge(self.sems[s], t)
                    inst = fn(eng)
                    inst.then_inc(self.sems[sem], inc)
            return body

        with self.nc.Block(name) as blk:
            for phys, eng_items in by_eng.items():
                getattr(blk, phys)(mk_body(eng_items))


def build_nc():
    nc = bass.Bass("TRN2", target_bir_lowering=False)

    x_own = nc.dram_tensor("x_own", [TOK, D], F32, kind="ExternalInput")
    x_peer = nc.dram_tensor("x_peer", [TOK, D], F32, kind="ExternalInput")
    flag_in = nc.dram_tensor("flag", [128, 1], F32, kind="ExternalInput")
    ld_in = nc.dram_tensor("ld", [1, NH], F32, kind="ExternalInput")
    wq_in = nc.dram_tensor("wq8", [NG, 128, KC * 256], F32, kind="ExternalInput")
    wk_in = nc.dram_tensor("wk8", [NG, 128, KC * 256], F32, kind="ExternalInput")
    wv_in = nc.dram_tensor("wv8", [NG, 128, KC * 256], F32, kind="ExternalInput")
    wo_in = nc.dram_tensor("wo8", [NG, 128, KC * 256], F32, kind="ExternalInput")
    wg1_in = nc.dram_tensor("wg1p", [128, KC * 128], F32, kind="ExternalInput")
    wg2_in = nc.dram_tensor("wg2p", [128, D], F32, kind="ExternalInput")
    normw_in = nc.dram_tensor("normw", [128, KC], F32, kind="ExternalInput")
    gatew_in = nc.dram_tensor("gatewp", [128, KC * 16], F32, kind="ExternalInput")
    w13_in = nc.dram_tensor("w13x", [NE, ICH, 128, KC * 256], F32, kind="ExternalInput")
    w2_in = nc.dram_tensor("w2x", [NE, 8, 128, ICH * 256], F32, kind="ExternalInput")
    out_t = nc.dram_tensor("out", [TOK, D], F32, kind="ExternalOutput")
    comb_dram = nc.dram_tensor("comb_scr", [16, HALF], F32, kind="Internal")

    dbgt = {}
    if DBG:
        for nm, shp in [("xnT", [128, KC * TOK]), ("S", [128, NH * 128]),
                        ("qf", [128, HPG * TOK]), ("kf", [128, HPG * TOK]),
                        ("onA", [128, KC * HALF]), ("yA", [128, KC * HALF]),
                        ("xn2A", [128, KC * HALF]), ("cA", [16, HALF]),
                        ("og", [128, NCH * HPG * 128])]:
            dbgt[nm] = nc.dram_tensor("dbg_" + nm, shp, F32, kind="ExternalOutput")

    x_own_v = x_own.ap().rearrange("(tc p) d -> tc p d", p=128)
    x_peer_v = x_peer.ap().rearrange("(tc p) d -> tc p d", p=128)
    out_v = out_t.ap().rearrange("(tc p) d -> tc p d", p=128)
    wq_v, wk_v, wv_v, wo_v = wq_in.ap(), wk_in.ap(), wv_in.ap(), wo_in.ap()
    w13_v, w2_v = w13_in.ap(), w2_in.ap()

    P = Prog(nc)
    st = contextlib.ExitStack()
    with st:
        E = st.enter_context
        P.init_sems(st)

        bank = [E(nc.psum_tensor(f"pb{i}", [128, 512], F32)) for i in range(6)]
        tbank = [E(nc.psum_tensor(f"pt{i}", [128, 1024], BF16)) for i in range(2)]
        BN = [f"b{i}" for i in range(6)]
        TBN = ["tb0", "tb1"]

        def dma(eng, dst, src, rd=(), wr=(), deps=()):
            return P.op(eng, lambda g, dst=dst, src=src: g.dma_start(dst, src),
                        rd=rd, wr=wr, deps=deps)

        R = dict(side="right")
        ones = E(nc.sbuf_tensor("ones", [128, 128], F32, **R))
        ones_b = E(nc.sbuf_tensor("ones_b", [128, 128], BF16, **R))
        idn_f = E(nc.sbuf_tensor("idn_f", [128, 128], F32, **R))
        idn_b = E(nc.sbuf_tensor("idn_b", [128, 128], BF16, **R))
        iota_fi = E(nc.sbuf_tensor("iota_fi", [128, 128], F32, **R))
        iota_f1 = E(nc.sbuf_tensor("iota_f1", [128, 128], F32, **R))
        iota_kd = E(nc.sbuf_tensor("iota_kd", [128, 1], F32, **R))
        ld_sb = E(nc.sbuf_tensor("ld_sb", [1, NH], F32, **R))
        ldb = E(nc.sbuf_tensor("ldb", [128, NH], F32, **R))
        cdb = E(nc.sbuf_tensor("cdb", [128, NH], F32, **R))
        kdb = E(nc.sbuf_tensor("kdb", [128, NH], F32, **R))
        flag = E(nc.sbuf_tensor("flag_sb", [128, 1], F32, **R))
        normw = E(nc.sbuf_tensor("normw_sb", [128, KC], F32, **R))
        gatewT = E(nc.sbuf_tensor("gatewT", [128, KC, 16], BF16, **R))
        dmp = E(nc.sbuf_tensor("dmp", [128, 128], BF16, **R))
        on_st1 = contextlib.ExitStack()
        onorm1 = on_st1.enter_context(nc.sbuf_tensor("onorm1", [128, KC, HALF], BF16, **R))
        on_st0 = contextlib.ExitStack()
        onorm0 = on_st0.enter_context(nc.sbuf_tensor("onorm0", [128, KC, HALF], BF16, **R))
        onorm = [onorm0, onorm1]

        at_st = contextlib.ExitStack()
        AE = at_st.enter_context
        wg2_sb = AE(nc.sbuf_tensor("wg2_sb", [128, D], BF16, **R))
        S_all = AE(nc.sbuf_tensor("S_all", [128, NH * 128], F32, **R))
        Dt_g = AE(nc.sbuf_tensor("Dt_g", [128, HPG * 128], F32, **R))
        qd_g = AE(nc.sbuf_tensor("qd_g", [128, HPG * 128], F32, **R))
        sdt_b = [AE(nc.sbuf_tensor(f"sdt{i}", [128, 128], BF16, **R)) for i in range(2)]
        kscb = [AE(nc.sbuf_tensor(f"kscb{i}", [128, 128], BF16, **R)) for i in range(2)]
        qs_b = [AE(nc.sbuf_tensor(f"qsb{i}", [128, 128], BF16, **R)) for i in range(2)]
        sbf_b = [AE(nc.sbuf_tensor(f"sbfb{i}", [128, 128], BF16, **R)) for i in range(2)]
        sg_b = [AE(nc.sbuf_tensor(f"sgb{i}", [128, 128], F32, **R)) for i in range(2)]
        on_b = [AE(nc.sbuf_tensor(f"onb{i}", [128, 128], BF16, **R)) for i in range(2)]
        rstd_b = AE(nc.sbuf_tensor("rstd_b", [128, 2], F32, **R))
        sdsc = AE(nc.sbuf_tensor("sdsc", [128, 2], F32, **R))
        wg1_st = contextlib.ExitStack()
        wg1_sb = wg1_st.enter_context(nc.sbuf_tensor("wg1_sb", [128, KC, 128], BF16, **R))

        SN = [f"S{h}" for h in range(NH)]

        dma("gpdma", ld_sb[:, :], ld_in.ap(), wr=["ld_sb"])
        dma("gpdma", flag[:, :], flag_in.ap(), wr=["flag"])
        dma("gpdma", normw[:, :], normw_in.ap(), wr=["normw"])
        dma("gpdma", gatewT[:, :, :], gatew_in.ap(), wr=["gatewT"])
        dma("gpdma", wg1_sb[:, :, :], wg1_in.ap(), wr=["wg1"])
        dma("gpdma", wg2_sb[:, :], wg2_in.ap(), wr=["wg2"])
        P.op("ge", lambda g: g.iota(iota_fi[:, :], [[1, 128]], base=0,
                                    channel_multiplier=-1,
                                    allow_small_or_imprecise_dtypes=True),
             wr=["iota_fi"])
        P.op("ge", lambda g: g.iota(iota_f1[:, :], [[1, 128]], base=1,
                                    channel_multiplier=0,
                                    allow_small_or_imprecise_dtypes=True),
             wr=["iota_f1"])
        P.op("ge", lambda g: g.iota(iota_kd[:, :], [[0, 1]], base=127,
                                    channel_multiplier=-1,
                                    allow_small_or_imprecise_dtypes=True),
             wr=["iota_kd"])
        P.op("ge", lambda g: g.memset(ones[:, :], 1.0), wr=["ones"])
        P.op("ge", lambda g: g.memset(S_all[:, :], 0.0), wr=SN)
        P.op("ge", lambda g: g.affine_select(idn_f[:, :], ones[:, :], [[1, 128]],
                                             compare_op=ALU.is_equal, fill=0.0,
                                             base=0, channel_multiplier=-1),
             rd=["ones"], wr=["idn_f"])
        P.op("ve", lambda v: v.tensor_copy(idn_b[:, :], idn_f[:, :]),
             rd=["idn_f"], wr=["idn_b"])
        P.op("ve", lambda v: v.tensor_copy(ones_b[:, :], ones[:, :]),
             rd=["ones"], wr=["ones_b"])

        def ldbcast_fn(t):
            for h in range(NH):
                inst = t.matmul(bank[5][:, h:h + 1], ones[0:1, :],
                                ld_sb[0:1, h:h + 1], start=True, stop=True)
            return inst
        P.op("pe", ldbcast_fn, rd=["ones", "ld_sb"], wr=["b5"])
        P.op("ve", lambda v: v.tensor_copy(ldb[:, :], bank[5][:, 0:NH]),
             rd=["b5"], wr=["ldb"])
        P.op("ve", lambda v: v.tensor_scalar(cdb[:, :], ldb[:, :], 128.0, None,
                                             ALU.mult), rd=["ldb"], wr=["cdbE"])
        P.op("se", lambda s: s.activation(cdb[:, :], cdb[:, :], AF.Exp),
             rd=["cdbE"], wr=["cdb"])

        def kdb_fn(s):
            for h in range(NH):
                inst = s.activation(kdb[:, h:h + 1], iota_kd[:, :], AF.Exp,
                                    scale=ldb[:, h:h + 1])
            return inst
        P.op("se", kdb_fn, rd=["ldb", "iota_kd"], wr=["kdb"])
        P.flush("setup")

        def emit_ln1(x_view, tc_lo, tc_hi, xnT, xname, bufs):
            xbuf, xn_tok, dump, lnst = bufs
            nxb = len(xbuf)
            for tc in range(tc_lo, tc_hi):
                i = tc - tc_lo
                xb, xbn = xbuf[i % nxb], f"xb{i % nxb}"
                ls, lsn = lnst[i % 2], f"lnst{i % 2}"
                xn, xnn = xn_tok[i % 2], f"xntok{i % 2}"
                dma("sydma", xb[:, :], x_view[tc], wr=[xbn])
                P.op("ve", lambda v, xb=xb, ls=ls: v.reduce_sum(
                    ls[:, 0:1], xb[:, :], axis=AX.X), rd=[xbn], wr=[lsn])
                P.op("se", lambda s, xb=xb, ls=ls: s.activation(
                    dump[:, :], xb[:, :], AF.Square, accum_out=ls[:, 1:2]),
                    rd=[xbn, lsn], wr=[lsn, "lndump"])
                def s_a(v, ls=ls):
                    v.tensor_scalar(ls[:, 2:3], ls[:, 0:1], 1.0 / D, None, ALU.mult)
                    return v.tensor_scalar(ls[:, 3:4], ls[:, 1:2], 1.0 / D, None,
                                           ALU.mult)
                P.op("ve", s_a, rd=[lsn], wr=[lsn])
                P.op("ve", lambda v, ls=ls: v.tensor_tensor(
                    ls[:, 4:5], ls[:, 2:3], ls[:, 2:3], ALU.mult), rd=[lsn], wr=[lsn])
                P.op("ve", lambda v, ls=ls: v.tensor_tensor(
                    ls[:, 5:6], ls[:, 3:4], ls[:, 4:5], ALU.subtract),
                    rd=[lsn], wr=[lsn])
                P.op("ve", lambda v, ls=ls: v.tensor_scalar(
                    ls[:, 5:6], ls[:, 5:6], EPS, None, ALU.add), rd=[lsn], wr=[lsn])
                P.op("se", lambda s, ls=ls: s.activation(
                    ls[:, 6:7], ls[:, 5:6], AF.Sqrt), rd=[lsn], wr=[lsn])
                P.op("ve", lambda v, ls=ls: v.reciprocal(ls[:, 7:8], ls[:, 6:7]),
                     rd=[lsn], wr=[lsn])
                P.op("ve", lambda v, xb=xb, ls=ls, xn=xn: v.tensor_scalar(
                    xn[:, :], xb[:, :], ls[:, 2:3], ls[:, 7:8],
                    ALU.subtract, ALU.mult), rd=[xbn, lsn], wr=[xnn])
                for h8 in range(2):
                    tb, tbn = tbank[h8], TBN[h8]
                    def tr_fn(t, xn=xn, h8=h8, tb=tb):
                        for k8 in range(8):
                            kc = h8 * 8 + k8
                            inst = t.transpose(tb[:, k8 * 128:(k8 + 1) * 128],
                                               xn[:, kc * 128:(kc + 1) * 128],
                                               idn_b[:, :])
                        return inst
                    P.op("pe", tr_fn, rd=[xnn, "idn_b"], wr=[tbn])
                    def ev_fn(v, i=i, h8=h8, tb=tb, xnT=xnT):
                        for k8 in range(8):
                            kc = h8 * 8 + k8
                            inst = v.tensor_copy(
                                xnT[:, kc, i * 128:(i + 1) * 128],
                                tb[:, k8 * 128:(k8 + 1) * 128])
                        return inst
                    P.op("ve", ev_fn, rd=[tbn], wr=[xname])

        with contextlib.ExitStack() as rst:
            RE = rst.enter_context
            xnpT = RE(nc.sbuf_tensor("xnpT", [128, KC, 512], BF16))
            ktok = RE(nc.sbuf_tensor("ktok", [128, 4, D], BF16))
            vtok = RE(nc.sbuf_tensor("vtok", [128, 4, D], BF16))
            wsl = [RE(nc.sbuf_tensor(f"rwsl{i}", [128, KC, 256], BF16))
                   for i in range(2)]
            xbuf = [RE(nc.sbuf_tensor(f"rxb{i}", [128, D], F32)) for i in range(2)]
            xn_tok = [RE(nc.sbuf_tensor(f"rxn{i}", [128, D], BF16)) for i in range(2)]
            dump = RE(nc.sbuf_tensor("rdump", [128, D], BF16))
            lnst = [RE(nc.sbuf_tensor(f"rlnst{i}", [128, 8], F32)) for i in range(2)]
            ksc = [RE(nc.sbuf_tensor(f"rksc{i}", [128, 128], BF16)) for i in range(2)]
            bufs = (xbuf, xn_tok, dump, lnst)

            for q in range(4):
                emit_ln1(x_peer_v, q * 4, (q + 1) * 4, xnpT, "xnpT", bufs)
                for wsrc, dstt, dsn in ((wk_v, ktok, "ktok"), (wv_v, vtok, "vtok")):
                    for es in range(8):
                        wb, wbn = wsl[es % 2], f"rwsl{es % 2}"
                        dma("gpdma", wb[:, :, :], wsrc[es], wr=[wbn])
                        for tc4 in range(4):
                            b = tc4 % 2
                            def mm_fn(t, tc4=tc4, wb=wb, b=b):
                                for kc in range(KC):
                                    inst = t.matmul(
                                        bank[b][:, 0:256],
                                        xnpT[:, kc, tc4 * 128:(tc4 + 1) * 128],
                                        wb[:, kc, :],
                                        start=(kc == 0), stop=(kc == KC - 1))
                                return inst
                            P.op("pe", mm_fn, rd=["xnpT", wbn], wr=[BN[b]])
                            P.op("se", lambda s, tc4=tc4, es=es, dstt=dstt, b=b:
                                 s.activation(dstt[:, tc4, es * 256:(es + 1) * 256],
                                              bank[b][:, 0:256], AF.Silu),
                                 rd=[BN[b]], wr=[f"{dsn}{tc4}"])
                for tc4 in range(4):
                    for h in range(NH):
                        kb = (tc4 * NH + h) % 2
                        P.op("ve", lambda v, tc4=tc4, h=h, kb=kb: v.tensor_scalar(
                            ksc[kb][:, :], ktok[:, tc4, h * 128:(h + 1) * 128],
                            kdb[:, h:h + 1], None, ALU.mult),
                            rd=[f"ktok{tc4}", "kdb"], wr=[f"rksc{kb}"])
                        P.op("pe", lambda t, tc4=tc4, h=h, kb=kb: t.matmul(
                            bank[2][:, 0:128], ksc[kb][:, :],
                            vtok[:, tc4, h * 128:(h + 1) * 128],
                            start=True, stop=True),
                            rd=[f"rksc{kb}", f"vtok{tc4}"], wr=["b2"])
                        P.op("ve", lambda v, h=h: v.scalar_tensor_tensor(
                            S_all[:, h * 128:(h + 1) * 128],
                            S_all[:, h * 128:(h + 1) * 128],
                            cdb[:, h:h + 1], bank[2][:, 0:128], ALU.mult, ALU.add),
                            rd=["b2", f"S{h}", "cdb"], wr=[f"S{h}"])
                P.flush(f"r_q{q}")
            P.op("ve", lambda v: v.tensor_scalar(S_all[:, :], S_all[:, :],
                                                 flag[:, :], None, ALU.mult),
                 rd=["flag"] + SN, wr=SN)
            P.flush("r_flag")

        ln_st = contextlib.ExitStack()
        LE = ln_st.enter_context
        xnT = LE(nc.sbuf_tensor("xnT", [128, KC, TOK], BF16))
        g1T = LE(nc.sbuf_tensor("g1T", [128, TOK], BF16))
        with contextlib.ExitStack() as lst:
            LE2 = lst.enter_context
            xbuf = [LE2(nc.sbuf_tensor(f"lxb{i}", [128, D], F32)) for i in range(3)]
            xn_tok = [LE2(nc.sbuf_tensor(f"lxn{i}", [128, D], BF16)) for i in range(2)]
            dump = LE2(nc.sbuf_tensor("ldump", [128, D], BF16))
            lnst = [LE2(nc.sbuf_tensor(f"llnst{i}", [128, 8], F32)) for i in range(2)]
            bufs = (xbuf, xn_tok, dump, lnst)
            emit_ln1(x_own_v, 0, NCH, xnT, "xnT", bufs)
            P.flush("l_ln1")

        for tt in range(4):
            b = tt % 2
            def mm_fn(t, tt=tt, b=b):
                for kc in range(KC):
                    inst = t.matmul(bank[b][:, 0:512], wg1_sb[:, kc, :],
                                    xnT[:, kc, tt * 512:(tt + 1) * 512],
                                    start=(kc == 0), stop=(kc == KC - 1))
                return inst
            P.op("pe", mm_fn, rd=["xnT", "wg1"], wr=[BN[b]])
            P.op("ve", lambda v, tt=tt, b=b: v.tensor_copy(
                g1T[:, tt * 512:(tt + 1) * 512], bank[b][:, 0:512]),
                rd=[BN[b]], wr=["g1T"])
        P.flush("l_g1")
        wg1_st.close()

        if DBG:
            with contextlib.ExitStack() as ds:
                stg = ds.enter_context(nc.sbuf_tensor("dbgstg", [128, TOK], F32))
                for kc in range(KC):
                    P.op("ve", lambda v, kc=kc: v.tensor_copy(stg[:, :], xnT[:, kc, :]),
                         rd=["xnT"], wr=["dstg"])
                    P.op("sydma", lambda sy, kc=kc: sy.dma_start(
                        dbgt["xnT"].ap()[:, kc * TOK:(kc + 1) * TOK], stg[:, :]),
                        rd=["dstg"], wr=["dstg"])
                P.flush("dbg_xnT")

        with contextlib.ExitStack() as gst:
            GE_ = gst.enter_context
            Qf = GE_(nc.sbuf_tensor("Qf", [128, HPG, TOK], BF16))
            Kf = GE_(nc.sbuf_tensor("Kf", [128, HPG, TOK], BF16))
            Vt = GE_(nc.sbuf_tensor("Vt", [128, NCH, HPG * 128], BF16))
            og_buf = GE_(nc.sbuf_tensor("og_buf", [128, NCH, HPG * 128], BF16))
            ms_buf = GE_(nc.sbuf_tensor("ms_buf", [128, NCH * HPG], F32))
            wgb = [GE_(nc.sbuf_tensor(f"wgb{i}", [128, KC, HPG * 128], BF16))
                   for i in range(2)]

            def gw_of(j):
                idx = 0 if j == 0 else 1
                return wgb[idx], f"wgb{idx}"

            for j, wsrc in enumerate((wq_v, wk_v)):
                gw, gwn = gw_of(j)
                dma("gpdma", gw[:, :, :], wsrc[0], wr=[gwn])

            for g_i in range(NG):
                for h2 in range(HPG):
                    gh = g_i * HPG + h2
                    P.op("se", lambda s, h2=h2, gh=gh: s.activation(
                        Dt_g[:, h2 * 128:(h2 + 1) * 128], iota_fi[:, :], AF.Exp,
                        scale=ldb[:, gh:gh + 1]),
                        rd=["iota_fi", "ldb"], wr=[f"DtE{h2}"])
                    P.op("ge", lambda g, h2=h2: g.affine_select(
                        Dt_g[:, h2 * 128:(h2 + 1) * 128],
                        Dt_g[:, h2 * 128:(h2 + 1) * 128], [[1, 128]],
                        compare_op=ALU.is_ge, fill=0.0, base=0,
                        channel_multiplier=-1),
                        rd=[f"DtE{h2}"], wr=[f"Dt{h2}"])
                    P.op("se", lambda s, h2=h2, gh=gh: s.activation(
                        qd_g[:, h2 * 128:(h2 + 1) * 128], iota_f1[:, :], AF.Exp,
                        scale=ldb[:, gh:gh + 1]),
                        rd=["iota_f1", "ldb"], wr=[f"qd{h2}"])
                for wb_wn, dst, dn in ((gw_of(0), Qf, "Qf"),
                                       (gw_of(1), Kf, "Kf")):
                    wb, wn = wb_wn
                    for h2 in range(HPG):
                        for tt in range(4):
                            b = (h2 * 4 + tt) % 2
                            def mm_fn(t, wb=wb, h2=h2, tt=tt, b=b):
                                for kc in range(KC):
                                    inst = t.matmul(
                                        bank[b][:, 0:512],
                                        wb[:, kc, h2 * 128:(h2 + 1) * 128],
                                        xnT[:, kc, tt * 512:(tt + 1) * 512],
                                        start=(kc == 0), stop=(kc == KC - 1))
                                return inst
                            P.op("pe", mm_fn, rd=[wn, "xnT"], wr=[BN[b]])
                            P.op("se", lambda s, dst=dst, h2=h2, tt=tt, b=b:
                                 s.activation(dst[:, h2, tt * 512:(tt + 1) * 512],
                                              bank[b][:, 0:512], AF.Silu),
                                 rd=[BN[b]], wr=[dn])
                wvb, wvn = gw_of(2)
                dma("gpdma", wvb[:, :, :], wv_v[g_i], wr=[wvn])
                for tc in range(NCH):
                    b = tc % 2
                    def mm_fn(t, tc=tc, b=b, wvb=wvb):
                        for kc in range(KC):
                            inst = t.matmul(bank[b][:, 0:256],
                                            xnT[:, kc, tc * 128:(tc + 1) * 128],
                                            wvb[:, kc, :],
                                            start=(kc == 0), stop=(kc == KC - 1))
                        return inst
                    P.op("pe", mm_fn, rd=[wvn, "xnT"], wr=[BN[b]])
                    P.op("se", lambda s, tc=tc, b=b: s.activation(
                        Vt[:, tc, :], bank[b][:, 0:256], AF.Silu),
                        rd=[BN[b]], wr=[f"Vt{tc}"])
                P.flush(f"g{g_i}_qkv")

                if g_i + 1 < NG:
                    for j, wsrc in enumerate((wq_v, wk_v)):
                        gw, gwn = gw_of(j)
                        dma("gpdma", gw[:, :, :], wsrc[g_i + 1], wr=[gwn])

                for cch in range(NCH):
                    for h2 in range(HPG):
                        gh = g_i * HPG + h2
                        rb = (cch * HPG + h2) % 2
                        P.op("pe", lambda t, h2=h2, cch=cch: t.matmul(
                            bank[2][:, 0:128], Kf[:, h2, cch * 128:(cch + 1) * 128],
                            Qf[:, h2, cch * 128:(cch + 1) * 128],
                            start=True, stop=True),
                            rd=["Kf", "Qf"], wr=["b2"])
                        P.op("pe", lambda t, h2=h2, cch=cch: t.transpose(
                            tbank[0][:, 0:128],
                            Kf[:, h2, cch * 128:(cch + 1) * 128], idn_b[:, :]),
                            rd=["Kf", "idn_b"], wr=["tb0"])
                        P.op("ve", lambda v, h2=h2, rb=rb: v.tensor_tensor(
                            sdt_b[rb][:, :], bank[2][:, 0:128],
                            Dt_g[:, h2 * 128:(h2 + 1) * 128], ALU.mult),
                            rd=["b2", f"Dt{h2}"], wr=[f"sdt{rb}"])
                        P.op("ve", lambda v, gh=gh, rb=rb: v.tensor_scalar(
                            kscb[rb][:, :], tbank[0][:, 0:128], kdb[:, gh:gh + 1],
                            None, ALU.mult), rd=["tb0", "kdb"], wr=[f"ksc{rb}"])
                        P.op("ve", lambda v, h2=h2, cch=cch, rb=rb: v.tensor_tensor(
                            qs_b[rb][:, :], Qf[:, h2, cch * 128:(cch + 1) * 128],
                            qd_g[:, h2 * 128:(h2 + 1) * 128], ALU.mult),
                            rd=["Qf", f"qd{h2}"], wr=[f"qs{rb}"])
                        P.op("se", lambda s, gh=gh, rb=rb: s.copy(
                            sbf_b[rb][:, :], S_all[:, gh * 128:(gh + 1) * 128]),
                            rd=[f"S{gh}"], wr=[f"sbf{rb}"])
                        P.op("pe", lambda t, cch=cch, h2=h2, rb=rb: t.matmul(
                            bank[3][:, 0:128], kscb[rb][:, :],
                            Vt[:, cch, h2 * 128:(h2 + 1) * 128],
                            start=True, stop=True),
                            rd=[f"ksc{rb}", f"Vt{cch}"], wr=["b3"])
                        def o_fn(t, cch=cch, h2=h2, rb=rb):
                            t.matmul(bank[0][:, 0:128], sdt_b[rb][:, :],
                                     Vt[:, cch, h2 * 128:(h2 + 1) * 128],
                                     start=True, stop=False)
                            return t.matmul(bank[0][:, 0:128], qs_b[rb][:, :],
                                            sbf_b[rb][:, :], start=False, stop=True)
                        P.op("pe", o_fn,
                             rd=[f"sdt{rb}", f"Vt{cch}", f"qs{rb}", f"sbf{rb}"],
                             wr=["b0"])
                        P.op("pe", lambda t, cch=cch, gh=gh: t.matmul(
                            bank[1][:, 0:128], g1T[:, cch * 128:(cch + 1) * 128],
                            wg2_sb[:, gh * 128:(gh + 1) * 128],
                            start=True, stop=True),
                            rd=["g1T", "wg2"], wr=["b1"])
                        P.op("ve", lambda v, gh=gh: v.scalar_tensor_tensor(
                            S_all[:, gh * 128:(gh + 1) * 128],
                            S_all[:, gh * 128:(gh + 1) * 128],
                            cdb[:, gh:gh + 1], bank[3][:, 0:128],
                            ALU.mult, ALU.add),
                            rd=["b3", f"S{gh}", "cdb"], wr=[f"S{gh}"])
                        P.op("se", lambda s, rb=rb: s.activation(
                            sg_b[rb][:, :], bank[1][:, 0:128], AF.Sigmoid),
                            rd=["b1"], wr=[f"sg{rb}"])
                        P.op("ve", lambda v, cch=cch, h2=h2, rb=rb: v.tensor_tensor(
                            og_buf[:, cch, h2 * 128:(h2 + 1) * 128],
                            bank[0][:, 0:128], sg_b[rb][:, :], ALU.mult),
                            rd=["b0", f"sg{rb}"], wr=[f"og{cch}"])
                        P.op("se", lambda s, cch=cch, h2=h2: s.activation(
                            dmp[:, :], og_buf[:, cch, h2 * 128:(h2 + 1) * 128],
                            AF.Square,
                            accum_out=ms_buf[:, cch * HPG + h2:cch * HPG + h2 + 1]),
                            rd=[f"og{cch}"], wr=[f"ms{cch}", "dmp"])
                P.flush(f"g{g_i}_scan")

                for cch in range(NCH):
                    h8 = cch // NHC
                    dsto = onorm[h8]
                    col = (cch % NHC) * 128
                    P.op("ve", lambda v, cch=cch: v.tensor_scalar(
                        sdsc[:, 0:2], ms_buf[:, cch * HPG:(cch + 1) * HPG],
                        1.0 / 128.0, EPS, ALU.mult, ALU.add),
                        rd=[f"ms{cch}"], wr=["sdscE"])
                    P.op("se", lambda s: s.activation(
                        sdsc[:, 0:2], sdsc[:, 0:2], AF.Sqrt),
                        rd=["sdscE"], wr=["sdsc"])
                    P.op("ve", lambda v: v.reciprocal(rstd_b[:, 0:2], sdsc[:, 0:2]),
                         rd=["sdsc"], wr=["rstdb"])
                    for h2 in range(HPG):
                        gh = g_i * HPG + h2
                        rb = (cch * HPG + h2) % 2
                        P.op("ve", lambda v, cch=cch, h2=h2, rb=rb: v.tensor_scalar(
                            on_b[rb][:, :], og_buf[:, cch, h2 * 128:(h2 + 1) * 128],
                            rstd_b[:, h2:h2 + 1], None, ALU.mult),
                            rd=[f"og{cch}", "rstdb"], wr=[f"on{rb}"])
                        P.op("pe", lambda t, rb=rb: t.transpose(
                            tbank[1][:, rb * 128:(rb + 1) * 128], on_b[rb][:, :],
                            idn_b[:, :]), rd=[f"on{rb}", "idn_b"], wr=["tb1"])
                        P.op("ve", lambda v, rb=rb, gh=gh, dsto=dsto, col=col, h8=h8:
                             v.tensor_scalar(dsto[:, gh, col:col + 128],
                                             tbank[1][:, rb * 128:(rb + 1) * 128],
                                             normw[:, gh:gh + 1], None, ALU.mult),
                             rd=["tb1", "normw"], wr=[f"on{h8}_{gh}"])
                P.flush(f"g{g_i}_rms")

            if DBG:
                P.op("sydma", lambda sy: sy.dma_start(dbgt["S"].ap(), S_all[:, :]),
                     rd=SN, wr=["dbgsdma"])
                with contextlib.ExitStack() as ds:
                    stq = ds.enter_context(nc.sbuf_tensor("dbgstq", [128, 1024], F32))
                    for h2 in range(HPG):
                        for hv in range(2):
                            P.op("ve", lambda v, h2=h2, hv=hv: v.tensor_copy(
                                stq[:, :], Qf[:, h2, hv * 1024:(hv + 1) * 1024]),
                                rd=["Qf"], wr=["dstq"])
                            P.op("sydma", lambda sy, h2=h2, hv=hv: sy.dma_start(
                                dbgt["qf"].ap()[:, h2 * TOK + hv * 1024:
                                                h2 * TOK + (hv + 1) * 1024],
                                stq[:, :]), rd=["dstq"], wr=["dstq"])
                            P.op("ve", lambda v, h2=h2, hv=hv: v.tensor_copy(
                                stq[:, :], Kf[:, h2, hv * 1024:(hv + 1) * 1024]),
                                rd=["Kf"], wr=["dstq"])
                            P.op("sydma", lambda sy, h2=h2, hv=hv: sy.dma_start(
                                dbgt["kf"].ap()[:, h2 * TOK + hv * 1024:
                                                h2 * TOK + (hv + 1) * 1024],
                                stq[:, :]), rd=["dstq"], wr=["dstq"])
                    P.flush("dbg_qf")
                with contextlib.ExitStack() as ds:
                    stg = ds.enter_context(nc.sbuf_tensor("dbgstg2", [128, 256], F32))
                    for cch in range(NCH):
                        P.op("ve", lambda v, cch=cch: v.tensor_copy(
                            stg[:, cch * 128:(cch + 1) * 128]
                            if False else stg[:, 0:256], og_buf[:, cch, :]),
                            rd=[f"og{cch}"], wr=["dstg2"])
                        P.op("sydma", lambda sy, cch=cch: sy.dma_start(
                            dbgt["og"].ap()[:, cch * 256:(cch + 1) * 256],
                            stg[:, 0:256]), rd=["dstg2"], wr=["dstg2"])
                    P.flush("dbg_og")

        ln_st.close()
        at_st.close()

        for hf in range(2):
            onrm = onorm[hf]
            onn = [f"on{hf}_{gh}" for gh in range(NH)]
            YN = [f"y{hf}_{o2}" for o2 in range(KC)]
            with contextlib.ExitStack() as hst:
                HE = hst.enter_context
                yacc = HE(nc.sbuf_tensor(f"yacc{hf}", [128, KC, HALF], F32))
                xn2T = HE(nc.sbuf_tensor(f"xn2T{hf}", [128, KC, HALF], BF16))

                with contextlib.ExitStack() as wst:
                    WE = wst.enter_context
                    xbuf = [WE(nc.sbuf_tensor(f"h{hf}xb{i}", [128, D], F32))
                            for i in range(2)]
                    wog = WE(nc.sbuf_tensor(f"wog{hf}", [128, KC, 256], BF16))
                    for tc8 in range(NHC):
                        tc = hf * NHC + tc8
                        xb, xbn = xbuf[tc8 % 2], f"hxb{tc8 % 2}"
                        dma("sydma", xb[:, :], x_own_v[tc], wr=[xbn])
                        for q in range(4):
                            b = 4 + (q % 2)
                            def tr_fn(t, xb=xb, q=q, b=b):
                                for k4 in range(4):
                                    kc = q * 4 + k4
                                    inst = t.transpose(
                                        bank[b][:, k4 * 128:(k4 + 1) * 128],
                                        xb[:, kc * 128:(kc + 1) * 128], idn_f[:, :])
                                return inst
                            P.op("pe", tr_fn, rd=[xbn, "idn_f"], wr=[BN[b]])
                            def ev_fn(v, tc8=tc8, q=q, b=b):
                                for k4 in range(4):
                                    kc = q * 4 + k4
                                    inst = v.tensor_copy(
                                        yacc[:, kc, tc8 * 128:(tc8 + 1) * 128],
                                        bank[b][:, k4 * 128:(k4 + 1) * 128])
                                return inst
                            P.op("ve", ev_fn, rd=[BN[b]],
                                 wr=[YN[q * 4 + k4] for k4 in range(4)])
                    P.flush(f"h{hf}_xt")

                    for sl in range(8):
                        dma("gpdma", wog[:, :, :], wo_v[sl], wr=["wog"])
                        for o2h in range(2):
                            o2 = sl * 2 + o2h
                            for tt in range(2):
                                b = (o2h * 2 + tt) % 2
                                def mm_fn(t, o2h=o2h, tt=tt, b=b):
                                    for kc in range(KC):
                                        inst = t.matmul(
                                            bank[b][:, 0:512],
                                            wog[:, kc, o2h * 128:(o2h + 1) * 128],
                                            onrm[:, kc, tt * 512:(tt + 1) * 512],
                                            start=(kc == 0), stop=(kc == KC - 1))
                                    return inst
                                P.op("pe", mm_fn, rd=["wog"] + onn, wr=[BN[b]])
                                P.op("ve", lambda v, o2=o2, tt=tt, b=b:
                                     v.tensor_tensor(
                                         yacc[:, o2, tt * 512:(tt + 1) * 512],
                                         yacc[:, o2, tt * 512:(tt + 1) * 512],
                                         bank[b][:, 0:512], ALU.add),
                                     rd=[BN[b], YN[o2]], wr=[YN[o2]])
                    P.flush(f"h{hf}_wo")

                if hf == 0:
                    on_st0.close()
                else:
                    on_st1.close()

                if DBG and hf == 0:
                    with contextlib.ExitStack() as ds:
                        stg = ds.enter_context(
                            nc.sbuf_tensor("dbgstg3", [128, HALF], F32))
                        for kc in range(KC):
                            P.op("ve", lambda v, kc=kc: v.tensor_copy(
                                stg[:, :], yacc[:, kc, :]), rd=[YN[kc]], wr=["ds3"])
                            P.op("sydma", lambda sy, kc=kc: sy.dma_start(
                                dbgt["yA"].ap()[:, kc * HALF:(kc + 1) * HALF],
                                stg[:, :]), rd=["ds3"], wr=["ds3"])
                        P.flush("dbg_yA")

                with contextlib.ExitStack() as lst2:
                    L2 = lst2.enter_context
                    ybf = [L2(nc.sbuf_tensor(f"h{hf}ybf{i}", [128, 512], BF16))
                           for i in range(2)]
                    r_m = L2(nc.sbuf_tensor(f"r_m{hf}", [1, 512], F32))
                    r_ms = L2(nc.sbuf_tensor(f"r_ms{hf}", [1, 512], F32))
                    r_t = L2(nc.sbuf_tensor(f"r_t{hf}", [1, 512], F32))
                    r_b = L2(nc.sbuf_tensor(f"r_b{hf}", [1, 512], F32))
                    ab = L2(nc.sbuf_tensor(f"ab{hf}", [128, 512], F32))
                    bb = L2(nc.sbuf_tensor(f"bb{hf}", [128, 512], F32))
                    t0b = L2(nc.sbuf_tensor(f"t0b{hf}", [128, 512], F32))
                    probs = L2(nc.sbuf_tensor(f"probs{hf}", [128, 16], F32))
                    ptmp = L2(nc.sbuf_tensor(f"ptmp{hf}", [128, 16], F32))
                    peq = L2(nc.sbuf_tensor(f"peq{hf}", [128, 16], F32))
                    pst = L2(nc.sbuf_tensor(f"pst{hf}", [128, 4], F32))
                    combT = L2(nc.sbuf_tensor(f"combT{hf}", [16, HALF], F32))

                    for tt in range(2):
                        for kc in range(KC):
                            rb = kc % 2
                            P.op("se", lambda s, kc=kc, tt=tt, rb=rb: s.copy(
                                ybf[rb][:, :], yacc[:, kc, tt * 512:(tt + 1) * 512]),
                                rd=[YN[kc]], wr=[f"ybf{rb}"])
                            P.op("pe", lambda t, kc=kc, rb=rb: t.matmul(
                                bank[5][0:1, 0:512], ones_b[:, 0:1], ybf[rb][:, :],
                                start=(kc == 0), stop=(kc == KC - 1)),
                                rd=[f"ybf{rb}", "ones_b"], wr=["b5"])
                        for kc in range(KC):
                            rb = kc % 2
                            P.op("se", lambda s, kc=kc, tt=tt, rb=rb: s.activation(
                                ybf[rb][:, :], yacc[:, kc, tt * 512:(tt + 1) * 512],
                                AF.Square), rd=[YN[kc]], wr=[f"ybf{rb}"])
                            P.op("pe", lambda t, kc=kc, rb=rb: t.matmul(
                                bank[4][0:1, 0:512], ones_b[:, 0:1], ybf[rb][:, :],
                                start=(kc == 0), stop=(kc == KC - 1)),
                                rd=[f"ybf{rb}", "ones_b"], wr=["b4"])
                        P.op("ve", lambda v: v.tensor_scalar(
                            r_m[0:1, :], bank[5][0:1, 0:512], 1.0 / D, None,
                            ALU.mult), rd=["b5"], wr=["r_m"])
                        P.op("ve", lambda v: v.tensor_scalar(
                            r_ms[0:1, :], bank[4][0:1, 0:512], 1.0 / D, None,
                            ALU.mult), rd=["b4"], wr=["r_ms"])
                        P.op("ve", lambda v: v.tensor_tensor(
                            r_t[0:1, :], r_m[0:1, :], r_m[0:1, :], ALU.mult),
                            rd=["r_m"], wr=["r_t"])
                        P.op("ve", lambda v: v.tensor_tensor(
                            r_ms[0:1, :], r_ms[0:1, :], r_t[0:1, :], ALU.subtract),
                            rd=["r_ms", "r_t"], wr=["r_ms"])
                        P.op("ve", lambda v: v.tensor_scalar(
                            r_ms[0:1, :], r_ms[0:1, :], EPS, None, ALU.add),
                            rd=["r_ms"], wr=["r_ms"])
                        P.op("se", lambda s: s.activation(
                            r_t[0:1, :], r_ms[0:1, :], AF.Sqrt),
                            rd=["r_ms"], wr=["r_t"])
                        P.op("ve", lambda v: v.reciprocal(r_ms[0:1, :], r_t[0:1, :]),
                             rd=["r_t"], wr=["r_ms"])
                        P.op("ve", lambda v: v.tensor_scalar(
                            r_t[0:1, :], r_ms[0:1, :], -1.0, None, ALU.mult),
                            rd=["r_ms"], wr=["r_t"])
                        P.op("ve", lambda v: v.tensor_tensor(
                            r_b[0:1, :], r_m[0:1, :], r_t[0:1, :], ALU.mult),
                            rd=["r_m", "r_t"], wr=["r_b"])
                        P.op("pe", lambda t: t.matmul(
                            bank[4][:, 0:512], ones[0:1, :], r_ms[0:1, :],
                            start=True, stop=True),
                            rd=["r_ms", "ones"], wr=["b4"])
                        P.op("ve", lambda v: v.tensor_copy(ab[:, :],
                                                           bank[4][:, 0:512]),
                             rd=["b4"], wr=["abf"])
                        P.op("pe", lambda t: t.matmul(
                            bank[4][:, 0:512], ones[0:1, :], r_b[0:1, :],
                            start=True, stop=True),
                            rd=["r_b", "ones"], wr=["b4"])
                        P.op("ve", lambda v: v.tensor_copy(bb[:, :],
                                                           bank[4][:, 0:512]),
                             rd=["b4"], wr=["bbf"])
                        for kc in range(KC):
                            P.op("ve", lambda v, kc=kc, tt=tt: v.tensor_tensor(
                                t0b[:, :], yacc[:, kc, tt * 512:(tt + 1) * 512],
                                ab[:, :], ALU.mult),
                                rd=[YN[kc], "abf"], wr=["t0b"])
                            P.op("ve", lambda v, kc=kc, tt=tt: v.tensor_tensor(
                                xn2T[:, kc, tt * 512:(tt + 1) * 512], t0b[:, :],
                                bb[:, :], ALU.add),
                                rd=["t0b", "bbf"], wr=["xn2T"])
                    P.flush(f"h{hf}_ln2")

                    if DBG and hf == 0:
                        with contextlib.ExitStack() as ds:
                            stg = ds.enter_context(
                                nc.sbuf_tensor("dbgstg4", [128, HALF], F32))
                            for kc in range(KC):
                                P.op("ve", lambda v, kc=kc: v.tensor_copy(
                                    stg[:, :], xn2T[:, kc, :]),
                                    rd=["xn2T"], wr=["ds4"])
                                P.op("sydma", lambda sy, kc=kc: sy.dma_start(
                                    dbgt["xn2A"].ap()[:, kc * HALF:(kc + 1) * HALF],
                                    stg[:, :]), rd=["ds4"], wr=["ds4"])
                            P.flush("dbg_xn2A")

                    for tc in range(NHC):
                        def lg_fn(t, tc=tc):
                            for kc in range(KC):
                                inst = t.matmul(bank[5][:, 0:16],
                                                xn2T[:, kc, tc * 128:(tc + 1) * 128],
                                                gatewT[:, kc, :],
                                                start=(kc == 0), stop=(kc == KC - 1))
                            return inst
                        P.op("pe", lg_fn, rd=["xn2T", "gatewT"], wr=["b5"])
                        P.op("ve", lambda v: v.reduce_max(
                            pst[:, 0:1], bank[5][:, 0:16], axis=AX.X, negate=True),
                            rd=["b5"], wr=["pst"])
                        P.op("se", lambda s: s.activation(
                            probs[:, :], bank[5][:, 0:16], AF.Exp,
                            bias=pst[:, 0:1], accum_out=pst[:, 1:2]),
                            rd=["b5", "pst"], wr=["probs", "pst"])
                        P.op("ve", lambda v: v.reciprocal(pst[:, 2:3], pst[:, 1:2]),
                             rd=["pst"], wr=["pst"])
                        P.op("ve", lambda v: v.tensor_scalar(
                            probs[:, :], probs[:, :], pst[:, 2:3], None, ALU.mult),
                            rd=["probs", "pst"], wr=["probs"])
                        P.op("ve", lambda v: v.tensor_copy(ptmp[:, :], probs[:, :]),
                             rd=["probs"], wr=["ptmp"])
                        for r in range(5):
                            P.op("ve", lambda v: v.reduce_max(
                                pst[:, 3:4], ptmp[:, :], axis=AX.X),
                                rd=["ptmp"], wr=["pst"])
                            P.op("ve", lambda v: v.tensor_scalar(
                                peq[:, :], ptmp[:, :], pst[:, 3:4], None, ALU.is_ge),
                                rd=["ptmp", "pst"], wr=["peq"])
                            P.op("ve", lambda v: v.scalar_tensor_tensor(
                                ptmp[:, :], peq[:, :], -1e9, ptmp[:, :],
                                ALU.mult, ALU.add),
                                rd=["peq", "ptmp"], wr=["ptmp"])
                        P.op("ve", lambda v: v.reduce_max(
                            pst[:, 3:4], ptmp[:, :], axis=AX.X),
                            rd=["ptmp"], wr=["pst"])
                        P.op("ve", lambda v: v.tensor_scalar(
                            peq[:, :], probs[:, :], pst[:, 3:4], None, ALU.is_ge),
                            rd=["probs", "pst"], wr=["peq"])
                        P.op("ve", lambda v: v.tensor_tensor(
                            probs[:, :], probs[:, :], peq[:, :], ALU.mult),
                            rd=["probs", "peq"], wr=["probs"])
                        P.op("ve", lambda v: v.reduce_sum(
                            pst[:, 3:4], probs[:, :], axis=AX.X),
                            rd=["probs"], wr=["pst"])
                        P.op("ve", lambda v: v.tensor_scalar(
                            pst[:, 3:4], pst[:, 3:4], 1e-20, None, ALU.add),
                            rd=["pst"], wr=["pst"])
                        P.op("ve", lambda v: v.reciprocal(pst[:, 2:3], pst[:, 3:4]),
                             rd=["pst"], wr=["pst"])
                        P.op("ve", lambda v: v.tensor_scalar(
                            probs[:, :], probs[:, :], pst[:, 2:3], None, ALU.mult),
                            rd=["probs", "pst"], wr=["probs"])
                        P.op("pe", lambda t: t.transpose(
                            bank[4][0:16, 0:128], probs[:, :], idn_f[:, :]),
                            rd=["probs", "idn_f"], wr=["b4"])
                        P.op("ve", lambda v, tc=tc: v.tensor_copy(
                            combT[:, tc * 128:(tc + 1) * 128], bank[4][0:16, 0:128]),
                            rd=["b4"], wr=["combT"])
                    dma("sydma", comb_dram.ap(), combT[:, :], rd=["combT"],
                        wr=["combdram"])
                    if DBG and hf == 0:
                        P.op("sydma", lambda sy: sy.dma_start(
                            dbgt["cA"].ap(), combT[:, :]), rd=["combT"],
                            wr=["dbgcdma"])
                    P.flush(f"h{hf}_gate")

                with contextlib.ExitStack() as fst:
                    FE = fst.enter_context
                    crow = FE(nc.sbuf_tensor(f"crow{hf}", [1, HALF], F32))
                    cbc = FE(nc.sbuf_tensor(f"cbc{hf}", [128, HALF], F32))
                    ws13 = [FE(nc.sbuf_tensor(f"h{hf}ws13_{i}", [128, KC, 256], BF16))
                            for i in range(2)]
                    ws2 = [FE(nc.sbuf_tensor(f"h{hf}ws2_{i}", [128, ICH, 256], BF16))
                           for i in range(2)]
                    hbuf = FE(nc.sbuf_tensor(f"hbuf{hf}", [128, ICH, HALF], BF16))
                    sg2 = [FE(nc.sbuf_tensor(f"h{hf}sg2_{i}", [128, 512], BF16))
                           for i in range(2)]
                    su2 = [FE(nc.sbuf_tensor(f"h{hf}su2_{i}", [128, 512], BF16))
                           for i in range(2)]
                    cd_flat = comb_dram.ap().rearrange("e t -> (e t)")

                    for e in range(NE):
                        if e < 16:
                            P.op("sydma", lambda sy, e=e: sy.dma_start(
                                crow[0:1, :], cd_flat[None, e * HALF:(e + 1) * HALF]),
                                rd=["combdram"], wr=["crow"])
                            for tt in range(2):
                                P.op("pe", lambda t, tt=tt: t.matmul(
                                    bank[4][:, 0:512], ones[0:1, :],
                                    crow[0:1, tt * 512:(tt + 1) * 512],
                                    start=True, stop=True),
                                    rd=["crow", "ones"], wr=["b4"])
                                P.op("ve", lambda v, tt=tt: v.tensor_copy(
                                    cbc[:, tt * 512:(tt + 1) * 512],
                                    bank[4][:, 0:512]), rd=["b4"], wr=["cbc"])
                        for j in range(ICH):
                            wb, wbn = ws13[j % 2], f"ws13_{j % 2}"
                            dma("gpdma", wb[:, :, :], w13_v[e, j], wr=[wbn])
                            for tt in range(2):
                                rb = tt
                                def g_fn(t, wb=wb, tt=tt):
                                    for kc in range(KC):
                                        inst = t.matmul(
                                            bank[0][:, 0:512], wb[:, kc, 0:128],
                                            xn2T[:, kc, tt * 512:(tt + 1) * 512],
                                            start=(kc == 0), stop=(kc == KC - 1))
                                    return inst
                                P.op("pe", g_fn, rd=[wbn, "xn2T"], wr=["b0"])
                                P.op("se", lambda s, rb=rb: s.activation(
                                    sg2[rb][:, :], bank[0][:, 0:512], AF.Silu),
                                    rd=["b0"], wr=[f"hsg{rb}"])
                                def u_fn(t, wb=wb, tt=tt):
                                    for kc in range(KC):
                                        inst = t.matmul(
                                            bank[1][:, 0:512], wb[:, kc, 128:256],
                                            xn2T[:, kc, tt * 512:(tt + 1) * 512],
                                            start=(kc == 0), stop=(kc == KC - 1))
                                    return inst
                                P.op("pe", u_fn, rd=[wbn, "xn2T"], wr=["b1"])
                                if e < 16:
                                    P.op("ve", lambda v, rb=rb, tt=tt: v.tensor_tensor(
                                        su2[rb][:, :], bank[1][:, 0:512],
                                        cbc[:, tt * 512:(tt + 1) * 512], ALU.mult),
                                        rd=["b1", "cbc"], wr=[f"hsu{rb}"])
                                else:
                                    P.op("ve", lambda v, rb=rb: v.tensor_copy(
                                        su2[rb][:, :], bank[1][:, 0:512]),
                                        rd=["b1"], wr=[f"hsu{rb}"])
                                P.op("ve", lambda v, j=j, tt=tt, rb=rb: v.tensor_tensor(
                                    hbuf[:, j, tt * 512:(tt + 1) * 512],
                                    sg2[rb][:, :], su2[rb][:, :], ALU.mult),
                                    rd=[f"hsg{rb}", f"hsu{rb}"], wr=[f"hb{j}"])
                        for sl in range(8):
                            wb, wbn = ws2[sl % 2], f"ws2_{sl % 2}"
                            dma("gpdma", wb[:, :, :], w2_v[e, sl], wr=[wbn])
                            for o2h in range(2):
                                o2 = sl * 2 + o2h
                                for tt in range(2):
                                    b = (o2h * 2 + tt) % 2
                                    def y_fn(t, wb=wb, o2h=o2h, tt=tt, b=b):
                                        for ic in range(ICH):
                                            inst = t.matmul(
                                                bank[b][:, 0:512],
                                                wb[:, ic, o2h * 128:(o2h + 1) * 128],
                                                hbuf[:, ic, tt * 512:(tt + 1) * 512],
                                                start=(ic == 0), stop=(ic == ICH - 1))
                                        return inst
                                    P.op("pe", y_fn,
                                         rd=[wbn] + [f"hb{ic}" for ic in range(ICH)],
                                         wr=[BN[b]])
                                    P.op("ve", lambda v, o2=o2, tt=tt, b=b:
                                         v.tensor_tensor(
                                             yacc[:, o2, tt * 512:(tt + 1) * 512],
                                             yacc[:, o2, tt * 512:(tt + 1) * 512],
                                             bank[b][:, 0:512], ALU.add),
                                         rd=[BN[b], YN[o2]], wr=[YN[o2]])
                        P.flush(f"h{hf}_ffn_e{e}")

                with contextlib.ExitStack() as ost:
                    OE = ost.enter_context
                    outst = [OE(nc.sbuf_tensor(f"h{hf}outst{i}", [128, D], F32))
                             for i in range(2)]
                    for tc8 in range(NHC):
                        tc = hf * NHC + tc8
                        ob, obn = outst[tc8 % 2], f"outst{tc8 % 2}"
                        for q in range(4):
                            b = 4 + (q % 2)
                            def tr_fn(t, q=q, b=b, tc8=tc8):
                                for k4 in range(4):
                                    kc = q * 4 + k4
                                    inst = t.transpose(
                                        bank[b][:, k4 * 128:(k4 + 1) * 128],
                                        yacc[:, kc, tc8 * 128:(tc8 + 1) * 128],
                                        idn_f[:, :])
                                return inst
                            P.op("pe", tr_fn,
                                 rd=[YN[q * 4 + k4] for k4 in range(4)] + ["idn_f"],
                                 wr=[BN[b]])
                            def ev_fn(v, q=q, b=b, ob=ob):
                                for k4 in range(4):
                                    kc = q * 4 + k4
                                    inst = v.tensor_copy(
                                        ob[:, kc * 128:(kc + 1) * 128],
                                        bank[b][:, k4 * 128:(k4 + 1) * 128])
                                return inst
                            P.op("ve", ev_fn, rd=[BN[b]], wr=[obn])
                        dma("sydma", out_v[tc], ob[:, :], rd=[obn])
                    P.flush(f"h{hf}_out")

        fins = [(P.sems[f"{q}{i}"], P.cnt[f"{q}{i}"])
                for q in ("dw", "dx") for i in range(8)]
        with nc.Block("fin") as blk:
            @blk.sync
            def _(sy):
                for s, t in fins:
                    if t:
                        sy.wait_ge(s, t)
    return nc



def _pack_cols(w, nslice, width):
    din, dout = w.shape
    kc = din // 128
    r = w.reshape(kc, 128, nslice, width).transpose(2, 1, 0, 3)
    return np.ascontiguousarray(r.reshape(nslice, 128, kc * width))


def _layouts(inputs):
    f = np.ascontiguousarray
    x = np.asarray(inputs["L0_attn_x"], dtype=np.float32)
    ld = f(np.asarray(inputs["L0_attn_log_decay"], np.float32).reshape(1, NH))
    wq = np.asarray(inputs["L0_attn_Wq"], np.float32)
    wk = np.asarray(inputs["L0_attn_Wk"], np.float32)
    wv = np.asarray(inputs["L0_attn_Wv"], np.float32)
    wo = np.asarray(inputs["L0_attn_Wo"], np.float32)
    wg1 = np.asarray(inputs["L0_attn_Wg1"], np.float32)
    wg2 = f(np.asarray(inputs["L0_attn_Wg2"], np.float32))
    normw = f(np.asarray(inputs["L0_attn_norm_w"], np.float32).reshape(KC, 128).T)
    gatew = np.asarray(inputs["L0_ffn_gate_w"], np.float32).T
    w13 = np.asarray(inputs["L0_ffn_w13"], np.float32)
    w2 = np.asarray(inputs["L0_ffn_w2"], np.float32)
    sw13 = np.asarray(inputs["L0_ffn_sw13"], np.float32)
    sw2 = np.asarray(inputs["L0_ffn_sw2"], np.float32)

    wq8 = _pack_cols(wq, NG, 256)
    wk8 = _pack_cols(wk, NG, 256)
    wv8 = _pack_cols(wv, NG, 256)
    wo8 = _pack_cols(wo, NG, 256)
    wg1p = f(wg1.reshape(KC, 128, 128).transpose(1, 0, 2).reshape(128, KC * 128))
    gatewp = f(gatew.reshape(KC, 128, 16).transpose(1, 0, 2).reshape(128, KC * 16))

    w13T = w13.transpose(0, 2, 1)
    w13T = w13T.reshape(16, D, 2, ICH, 128).transpose(0, 1, 3, 2, 4).reshape(16, D, 2816)
    sw13T = sw13.T.reshape(D, 2, 22, 128).transpose(0, 2, 1, 3).reshape(D, 5632)
    w13all = np.concatenate([w13T, sw13T[None, :, 0:2816],
                             sw13T[None, :, 2816:5632]], axis=0)
    w13x = w13all.reshape(NE, KC, 128, ICH, 256).transpose(0, 3, 2, 1, 4)
    w13x = f(w13x.reshape(NE, ICH, 128, KC * 256))

    w2T = w2.transpose(0, 2, 1)
    sw2T = sw2.T
    w2all = np.concatenate([w2T, sw2T[None, 0:1408, :], sw2T[None, 1408:2816, :]],
                           axis=0)
    w2x = w2all.reshape(NE, ICH, 128, 8, 256).transpose(0, 3, 2, 1, 4)
    w2x = f(w2x.reshape(NE, 8, 128, ICH * 256))

    return dict(x=x, ld=ld, wq8=wq8, wk8=wk8, wv8=wv8, wo8=wo8, wg1p=wg1p,
                wg2p=wg2, normw=normw, gatewp=gatewp, w13x=w13x, w2x=w2x)


_NC_CACHE = {}
_LAST_RES = None


def kernel(**inputs):
    L = _layouts(inputs)
    x = L["x"].reshape(4, 2, TOK, D)
    flag1 = np.ones((128, 1), np.float32)
    flag0 = np.zeros((128, 1), np.float32)
    in_maps = []
    for core in range(8):
        b, hf = core // 2, core % 2
        in_maps.append(dict(
            x_own=np.ascontiguousarray(x[b, hf]),
            x_peer=np.ascontiguousarray(x[b, 0]),
            flag=(flag1 if hf == 1 else flag0),
            ld=L["ld"], wq8=L["wq8"], wk8=L["wk8"], wv8=L["wv8"], wo8=L["wo8"],
            wg1p=L["wg1p"], wg2p=L["wg2p"], normw=L["normw"], gatewp=L["gatewp"],
            w13x=L["w13x"], w2x=L["w2x"]))
    if "nc" not in _NC_CACHE:
        _NC_CACHE["nc"] = build_nc()
    nc = _NC_CACHE["nc"]
    res = run_bass_kernel_spmd(nc, in_maps, core_ids=list(range(8)))
    global _LAST_RES
    _LAST_RES = res
    out = np.empty((4, 2, TOK, D), np.float32)
    for core in range(8):
        b, hf = core // 2, core % 2
        out[b, hf] = res.results[core]["out"]
    return out.reshape(4, 4096, D)


# version 04
# speedup vs baseline: 1.0179x; 1.0179x over previous
import contextlib
import numpy as np
import ml_dtypes
import concourse.bass as bass
import concourse.mybir as mybir
from concourse.bass_utils import run_bass_kernel_spmd

F32 = mybir.dt.float32
BF16 = mybir.dt.bfloat16
AF = mybir.ActivationFunctionType
ALU = mybir.AluOpType
AX = mybir.AxisListType

D = 2048
KC = 16
TOK = 2048
NCH = 16
HALF = 1024
NHC = 8
NH = 16
HPG = 2
NG = NH // HPG
NE = 18
ICH = 11
EPS = 1e-5

DBG = False

ENG_SEM = {"pe": "pe", "ve": "ve", "se": "se", "ge": "ge",
           "sydma": "dx", "gpdma": "dw"}
ENG_PHYS = {"pe": "tensor", "ve": "vector", "se": "scalar", "ge": "gpsimd",
            "sydma": "sync", "gpdma": "gpsimd"}


class Prog:
    def __init__(self, nc):
        self.nc = nc
        self.sems = {}
        self.cnt = {}
        self.items = []
        self.last_w = {}
        self.readers = {}
        self.waited = {}

    def init_sems(self, stack):
        for s in ["pe", "ve", "se", "ge"]:
            self.sems[s] = stack.enter_context(self.nc.semaphore(s))
            self.cnt[s] = 0
        for q in ["dx", "dw"]:
            for i in range(8):
                s = f"{q}{i}"
                self.sems[s] = stack.enter_context(self.nc.semaphore(s))
                self.cnt[s] = 0
        self.dma_rr = {"dx": 0, "dw": 0}

    def op(self, eng, fn, rd=(), wr=(), deps=()):
        sem = ENG_SEM[eng]
        phys = ENG_PHYS[eng]
        if sem in ("dx", "dw"):
            inc = 16
            i = self.dma_rr[sem]
            self.dma_rr[sem] = (i + 1) % 8
            sem = f"{sem}{i}"
        else:
            inc = 1
        need = {}

        def add(d):
            if d is None:
                return
            s, t = d
            if need.get(s, -1) < t:
                need[s] = t

        for r in rd:
            add(self.last_w.get(r))
        for r in wr:
            add(self.last_w.get(r))
            for s, t in self.readers.get(r, {}).items():
                add((s, t))
        for d in deps:
            add(d)
        w = self.waited.setdefault(phys, {})
        waits = {}
        for s, t in sorted(need.items()):
            if w.get(s, -1) >= t:
                continue
            waits[s] = t
            w[s] = t
        self.cnt[sem] += inc
        tgt = self.cnt[sem]
        label = (sem, tgt)
        for r in rd:
            self.readers.setdefault(r, {})
            if self.readers[r].get(sem, -1) < tgt:
                self.readers[r][sem] = tgt
        for r in wr:
            self.last_w[r] = label
            self.readers[r] = {}
        self.items.append((phys, fn, waits, sem, inc))
        return label

    def flush(self, name):
        if not self.items:
            return
        items = self.items
        self.items = []
        by_eng = {}
        for it in items:
            by_eng.setdefault(it[0], []).append(it)

        def mk_body(eng_items):
            def body(eng):
                for _, fn, waits, sem, inc in eng_items:
                    for s, t in waits.items():
                        eng.wait_ge(self.sems[s], t)
                    inst = fn(eng)
                    inst.then_inc(self.sems[sem], inc)
            return body

        with self.nc.Block(name) as blk:
            for phys, eng_items in by_eng.items():
                getattr(blk, phys)(mk_body(eng_items))


def build_nc():
    nc = bass.Bass("TRN2", target_bir_lowering=False)

    x_own = nc.dram_tensor("x_own", [TOK, D], F32, kind="ExternalInput")
    x_peer = nc.dram_tensor("x_peer", [TOK, D], F32, kind="ExternalInput")
    flag_in = nc.dram_tensor("flag", [128, 1], F32, kind="ExternalInput")
    ld_in = nc.dram_tensor("ld", [1, NH], F32, kind="ExternalInput")
    wq_in = nc.dram_tensor("wq8", [NG, 128, KC * 256], F32, kind="ExternalInput")
    wk_in = nc.dram_tensor("wk8", [NG, 128, KC * 256], F32, kind="ExternalInput")
    wv_in = nc.dram_tensor("wv8", [NG, 128, KC * 256], F32, kind="ExternalInput")
    wo_in = nc.dram_tensor("wo8", [NG, 128, KC * 256], F32, kind="ExternalInput")
    wg1_in = nc.dram_tensor("wg1p", [128, KC * 128], F32, kind="ExternalInput")
    wg2_in = nc.dram_tensor("wg2p", [128, D], F32, kind="ExternalInput")
    normw_in = nc.dram_tensor("normw", [128, KC], F32, kind="ExternalInput")
    gatew_in = nc.dram_tensor("gatewp", [128, KC * 16], F32, kind="ExternalInput")
    w13_in = nc.dram_tensor("w13x", [NE, ICH, 128, KC * 256], F32, kind="ExternalInput")
    w2_in = nc.dram_tensor("w2x", [NE, 8, 128, ICH * 256], F32, kind="ExternalInput")
    out_t = nc.dram_tensor("out", [TOK, D], F32, kind="ExternalOutput")
    comb_dram = nc.dram_tensor("comb_scr", [16, HALF], F32, kind="Internal")

    dbgt = {}
    if DBG:
        for nm, shp in [("xnT", [128, KC * TOK]), ("S", [128, NH * 128]),
                        ("qf", [128, HPG * TOK]), ("kf", [128, HPG * TOK]),
                        ("onA", [128, KC * HALF]), ("yA", [128, KC * HALF]),
                        ("xn2A", [128, KC * HALF]), ("cA", [16, HALF]),
                        ("og", [128, NCH * HPG * 128])]:
            dbgt[nm] = nc.dram_tensor("dbg_" + nm, shp, F32, kind="ExternalOutput")

    x_own_v = x_own.ap().rearrange("(tc p) d -> tc p d", p=128)
    x_peer_v = x_peer.ap().rearrange("(tc p) d -> tc p d", p=128)
    out_v = out_t.ap().rearrange("(tc p) d -> tc p d", p=128)
    wq_v, wk_v, wv_v, wo_v = wq_in.ap(), wk_in.ap(), wv_in.ap(), wo_in.ap()
    w13_v, w2_v = w13_in.ap(), w2_in.ap()

    P = Prog(nc)
    st = contextlib.ExitStack()
    with st:
        E = st.enter_context
        P.init_sems(st)

        bank = [E(nc.psum_tensor(f"pb{i}", [128, 512], F32)) for i in range(6)]
        tbank = [E(nc.psum_tensor(f"pt{i}", [128, 1024], BF16)) for i in range(2)]
        BN = [f"b{i}" for i in range(6)]
        TBN = ["tb0", "tb1"]

        def dma(eng, dst, src, rd=(), wr=(), deps=()):
            return P.op(eng, lambda g, dst=dst, src=src: g.dma_start(dst, src),
                        rd=rd, wr=wr, deps=deps)

        R = dict(side="right")
        ones = E(nc.sbuf_tensor("ones", [128, 128], F32, **R))
        ones_b = E(nc.sbuf_tensor("ones_b", [128, 128], BF16, **R))
        idn_f = E(nc.sbuf_tensor("idn_f", [128, 128], F32, **R))
        idn_b = E(nc.sbuf_tensor("idn_b", [128, 128], BF16, **R))
        iota_fi = E(nc.sbuf_tensor("iota_fi", [128, 128], F32, **R))
        iota_f1 = E(nc.sbuf_tensor("iota_f1", [128, 128], F32, **R))
        iota_kd = E(nc.sbuf_tensor("iota_kd", [128, 1], F32, **R))
        ld_sb = E(nc.sbuf_tensor("ld_sb", [1, NH], F32, **R))
        ldb = E(nc.sbuf_tensor("ldb", [128, NH], F32, **R))
        cdb = E(nc.sbuf_tensor("cdb", [128, NH], F32, **R))
        kdb = E(nc.sbuf_tensor("kdb", [128, NH], F32, **R))
        flag = E(nc.sbuf_tensor("flag_sb", [128, 1], F32, **R))
        normw = E(nc.sbuf_tensor("normw_sb", [128, KC], F32, **R))
        gatewT = E(nc.sbuf_tensor("gatewT", [128, KC, 16], BF16, **R))
        dmp = E(nc.sbuf_tensor("dmp", [128, 128], BF16, **R))
        on_st1 = contextlib.ExitStack()
        onorm1 = on_st1.enter_context(nc.sbuf_tensor("onorm1", [128, KC, HALF], BF16, **R))
        on_st0 = contextlib.ExitStack()
        onorm0 = on_st0.enter_context(nc.sbuf_tensor("onorm0", [128, KC, HALF], BF16, **R))
        onorm = [onorm0, onorm1]

        at_st = contextlib.ExitStack()
        AE = at_st.enter_context
        wg2_sb = AE(nc.sbuf_tensor("wg2_sb", [128, D], BF16, **R))
        S_all = AE(nc.sbuf_tensor("S_all", [128, NH * 128], F32, **R))
        Dt_g = AE(nc.sbuf_tensor("Dt_g", [128, HPG * 128], F32, **R))
        qd_g = AE(nc.sbuf_tensor("qd_g", [128, HPG * 128], F32, **R))
        sdt_b = [AE(nc.sbuf_tensor(f"sdt{i}", [128, 128], BF16, **R)) for i in range(2)]
        kscb = [AE(nc.sbuf_tensor(f"kscb{i}", [128, 128], BF16, **R)) for i in range(2)]
        qs_b = [AE(nc.sbuf_tensor(f"qsb{i}", [128, 128], BF16, **R)) for i in range(2)]
        sbf_b = [AE(nc.sbuf_tensor(f"sbfb{i}", [128, 128], BF16, **R)) for i in range(2)]
        sg_b = [AE(nc.sbuf_tensor(f"sgb{i}", [128, 128], F32, **R)) for i in range(2)]
        on_b = [AE(nc.sbuf_tensor(f"onb{i}", [128, 128], BF16, **R)) for i in range(2)]
        rstd_b = AE(nc.sbuf_tensor("rstd_b", [128, 2], F32, **R))
        sdsc = AE(nc.sbuf_tensor("sdsc", [128, 2], F32, **R))
        wg1_st = contextlib.ExitStack()
        wg1_sb = wg1_st.enter_context(nc.sbuf_tensor("wg1_sb", [128, KC, 128], BF16, **R))

        SN = [f"S{h}" for h in range(NH)]

        dma("gpdma", ld_sb[:, :], ld_in.ap(), wr=["ld_sb"])
        dma("gpdma", flag[:, :], flag_in.ap(), wr=["flag"])
        dma("gpdma", normw[:, :], normw_in.ap(), wr=["normw"])
        dma("gpdma", gatewT[:, :, :], gatew_in.ap(), wr=["gatewT"])
        dma("gpdma", wg1_sb[:, :, :], wg1_in.ap(), wr=["wg1"])
        dma("gpdma", wg2_sb[:, :], wg2_in.ap(), wr=["wg2"])
        P.op("ge", lambda g: g.iota(iota_fi[:, :], [[1, 128]], base=0,
                                    channel_multiplier=-1,
                                    allow_small_or_imprecise_dtypes=True),
             wr=["iota_fi"])
        P.op("ge", lambda g: g.iota(iota_f1[:, :], [[1, 128]], base=1,
                                    channel_multiplier=0,
                                    allow_small_or_imprecise_dtypes=True),
             wr=["iota_f1"])
        P.op("ge", lambda g: g.iota(iota_kd[:, :], [[0, 1]], base=127,
                                    channel_multiplier=-1,
                                    allow_small_or_imprecise_dtypes=True),
             wr=["iota_kd"])
        P.op("ge", lambda g: g.memset(ones[:, :], 1.0), wr=["ones"])
        P.op("ge", lambda g: g.memset(S_all[:, :], 0.0), wr=SN)
        P.op("ge", lambda g: g.affine_select(idn_f[:, :], ones[:, :], [[1, 128]],
                                             compare_op=ALU.is_equal, fill=0.0,
                                             base=0, channel_multiplier=-1),
             rd=["ones"], wr=["idn_f"])
        P.op("ve", lambda v: v.tensor_copy(idn_b[:, :], idn_f[:, :]),
             rd=["idn_f"], wr=["idn_b"])
        P.op("ve", lambda v: v.tensor_copy(ones_b[:, :], ones[:, :]),
             rd=["ones"], wr=["ones_b"])

        def ldbcast_fn(t):
            for h in range(NH):
                inst = t.matmul(bank[5][:, h:h + 1], ones[0:1, :],
                                ld_sb[0:1, h:h + 1], start=True, stop=True)
            return inst
        P.op("pe", ldbcast_fn, rd=["ones", "ld_sb"], wr=["b5"])
        P.op("ve", lambda v: v.tensor_copy(ldb[:, :], bank[5][:, 0:NH]),
             rd=["b5"], wr=["ldb"])
        P.op("ve", lambda v: v.tensor_scalar(cdb[:, :], ldb[:, :], 128.0, None,
                                             ALU.mult), rd=["ldb"], wr=["cdbE"])
        P.op("se", lambda s: s.activation(cdb[:, :], cdb[:, :], AF.Exp),
             rd=["cdbE"], wr=["cdb"])

        def kdb_fn(s):
            for h in range(NH):
                inst = s.activation(kdb[:, h:h + 1], iota_kd[:, :], AF.Exp,
                                    scale=ldb[:, h:h + 1])
            return inst
        P.op("se", kdb_fn, rd=["ldb", "iota_kd"], wr=["kdb"])
        P.flush("setup")

        def emit_ln1(x_view, tc_lo, tc_hi, xnT, xname, bufs):
            xbuf, xn_tok, dump, lnst = bufs
            nxb = len(xbuf)
            for tc in range(tc_lo, tc_hi):
                i = tc - tc_lo
                xb, xbn = xbuf[i % nxb], f"xb{i % nxb}"
                ls, lsn = lnst[i % 2], f"lnst{i % 2}"
                xn, xnn = xn_tok[i % 2], f"xntok{i % 2}"
                dma("sydma", xb[:, :], x_view[tc], wr=[xbn])
                P.op("ve", lambda v, xb=xb, ls=ls: v.reduce_sum(
                    ls[:, 0:1], xb[:, :], axis=AX.X), rd=[xbn], wr=[lsn])
                P.op("se", lambda s, xb=xb, ls=ls: s.activation(
                    dump[:, :], xb[:, :], AF.Square, accum_out=ls[:, 1:2]),
                    rd=[xbn, lsn], wr=[lsn, "lndump"])
                def s_a(v, ls=ls):
                    v.tensor_scalar(ls[:, 2:3], ls[:, 0:1], 1.0 / D, None, ALU.mult)
                    return v.tensor_scalar(ls[:, 3:4], ls[:, 1:2], 1.0 / D, None,
                                           ALU.mult)
                P.op("ve", s_a, rd=[lsn], wr=[lsn])
                P.op("ve", lambda v, ls=ls: v.tensor_tensor(
                    ls[:, 4:5], ls[:, 2:3], ls[:, 2:3], ALU.mult), rd=[lsn], wr=[lsn])
                P.op("ve", lambda v, ls=ls: v.tensor_tensor(
                    ls[:, 5:6], ls[:, 3:4], ls[:, 4:5], ALU.subtract),
                    rd=[lsn], wr=[lsn])
                P.op("ve", lambda v, ls=ls: v.tensor_scalar(
                    ls[:, 5:6], ls[:, 5:6], EPS, None, ALU.add), rd=[lsn], wr=[lsn])
                P.op("se", lambda s, ls=ls: s.activation(
                    ls[:, 6:7], ls[:, 5:6], AF.Sqrt), rd=[lsn], wr=[lsn])
                P.op("ve", lambda v, ls=ls: v.reciprocal(ls[:, 7:8], ls[:, 6:7]),
                     rd=[lsn], wr=[lsn])
                P.op("ve", lambda v, xb=xb, ls=ls, xn=xn: v.tensor_scalar(
                    xn[:, :], xb[:, :], ls[:, 2:3], ls[:, 7:8],
                    ALU.subtract, ALU.mult), rd=[xbn, lsn], wr=[xnn])
                for h8 in range(2):
                    tb, tbn = tbank[h8], TBN[h8]
                    def tr_fn(t, xn=xn, h8=h8, tb=tb):
                        for k8 in range(8):
                            kc = h8 * 8 + k8
                            inst = t.transpose(tb[:, k8 * 128:(k8 + 1) * 128],
                                               xn[:, kc * 128:(kc + 1) * 128],
                                               idn_b[:, :])
                        return inst
                    P.op("pe", tr_fn, rd=[xnn, "idn_b"], wr=[tbn])
                    def ev_fn(v, i=i, h8=h8, tb=tb, xnT=xnT):
                        for k8 in range(8):
                            kc = h8 * 8 + k8
                            inst = v.tensor_copy(
                                xnT[:, kc, i * 128:(i + 1) * 128],
                                tb[:, k8 * 128:(k8 + 1) * 128])
                        return inst
                    P.op("ve", ev_fn, rd=[tbn], wr=[xname])

        with contextlib.ExitStack() as rst:
            RE = rst.enter_context
            xnpT = RE(nc.sbuf_tensor("xnpT", [128, KC, 512], BF16))
            ktok = RE(nc.sbuf_tensor("ktok", [128, 4, D], BF16))
            vtok = RE(nc.sbuf_tensor("vtok", [128, 4, D], BF16))
            wsl = [RE(nc.sbuf_tensor(f"rwsl{i}", [128, KC, 256], BF16))
                   for i in range(2)]
            xbuf = [RE(nc.sbuf_tensor(f"rxb{i}", [128, D], F32)) for i in range(2)]
            xn_tok = [RE(nc.sbuf_tensor(f"rxn{i}", [128, D], BF16)) for i in range(2)]
            dump = RE(nc.sbuf_tensor("rdump", [128, D], BF16))
            lnst = [RE(nc.sbuf_tensor(f"rlnst{i}", [128, 8], F32)) for i in range(2)]
            ksc = [RE(nc.sbuf_tensor(f"rksc{i}", [128, 128], BF16)) for i in range(2)]
            bufs = (xbuf, xn_tok, dump, lnst)

            for q in range(4):
                emit_ln1(x_peer_v, q * 4, (q + 1) * 4, xnpT, "xnpT", bufs)
                for wsrc, dstt, dsn in ((wk_v, ktok, "ktok"), (wv_v, vtok, "vtok")):
                    for es in range(8):
                        wb, wbn = wsl[es % 2], f"rwsl{es % 2}"
                        dma("gpdma", wb[:, :, :], wsrc[es], wr=[wbn])
                        for tc4 in range(4):
                            b = tc4 % 2
                            def mm_fn(t, tc4=tc4, wb=wb, b=b):
                                for kc in range(KC):
                                    inst = t.matmul(
                                        bank[b][:, 0:256],
                                        xnpT[:, kc, tc4 * 128:(tc4 + 1) * 128],
                                        wb[:, kc, :],
                                        start=(kc == 0), stop=(kc == KC - 1))
                                return inst
                            P.op("pe", mm_fn, rd=["xnpT", wbn], wr=[BN[b]])
                            P.op("se", lambda s, tc4=tc4, es=es, dstt=dstt, b=b:
                                 s.activation(dstt[:, tc4, es * 256:(es + 1) * 256],
                                              bank[b][:, 0:256], AF.Silu),
                                 rd=[BN[b]], wr=[f"{dsn}{tc4}"])
                for tc4 in range(4):
                    for h in range(NH):
                        kb = (tc4 * NH + h) % 2
                        P.op("ve", lambda v, tc4=tc4, h=h, kb=kb: v.tensor_scalar(
                            ksc[kb][:, :], ktok[:, tc4, h * 128:(h + 1) * 128],
                            kdb[:, h:h + 1], None, ALU.mult),
                            rd=[f"ktok{tc4}", "kdb"], wr=[f"rksc{kb}"])
                        P.op("pe", lambda t, tc4=tc4, h=h, kb=kb: t.matmul(
                            bank[2][:, 0:128], ksc[kb][:, :],
                            vtok[:, tc4, h * 128:(h + 1) * 128],
                            start=True, stop=True),
                            rd=[f"rksc{kb}", f"vtok{tc4}"], wr=["b2"])
                        P.op("ve", lambda v, h=h: v.scalar_tensor_tensor(
                            S_all[:, h * 128:(h + 1) * 128],
                            S_all[:, h * 128:(h + 1) * 128],
                            cdb[:, h:h + 1], bank[2][:, 0:128], ALU.mult, ALU.add),
                            rd=["b2", f"S{h}", "cdb"], wr=[f"S{h}"])
            P.op("ve", lambda v: v.tensor_scalar(S_all[:, :], S_all[:, :],
                                                 flag[:, :], None, ALU.mult),
                 rd=["flag"] + SN, wr=SN)
            P.flush("r_flag")

        ln_st = contextlib.ExitStack()
        LE = ln_st.enter_context
        xnT = LE(nc.sbuf_tensor("xnT", [128, KC, TOK], BF16))
        g1T = LE(nc.sbuf_tensor("g1T", [128, TOK], BF16))
        with contextlib.ExitStack() as lst:
            LE2 = lst.enter_context
            xbuf = [LE2(nc.sbuf_tensor(f"lxb{i}", [128, D], F32)) for i in range(3)]
            xn_tok = [LE2(nc.sbuf_tensor(f"lxn{i}", [128, D], BF16)) for i in range(2)]
            dump = LE2(nc.sbuf_tensor("ldump", [128, D], BF16))
            lnst = [LE2(nc.sbuf_tensor(f"llnst{i}", [128, 8], F32)) for i in range(2)]
            bufs = (xbuf, xn_tok, dump, lnst)
            emit_ln1(x_own_v, 0, NCH, xnT, "xnT", bufs)
            P.flush("l_ln1")

        for tt in range(4):
            b = tt % 2
            def mm_fn(t, tt=tt, b=b):
                for kc in range(KC):
                    inst = t.matmul(bank[b][:, 0:512], wg1_sb[:, kc, :],
                                    xnT[:, kc, tt * 512:(tt + 1) * 512],
                                    start=(kc == 0), stop=(kc == KC - 1))
                return inst
            P.op("pe", mm_fn, rd=["xnT", "wg1"], wr=[BN[b]])
            P.op("ve", lambda v, tt=tt, b=b: v.tensor_copy(
                g1T[:, tt * 512:(tt + 1) * 512], bank[b][:, 0:512]),
                rd=[BN[b]], wr=["g1T"])
        P.flush("l_g1")
        wg1_st.close()

        if DBG:
            with contextlib.ExitStack() as ds:
                stg = ds.enter_context(nc.sbuf_tensor("dbgstg", [128, TOK], F32))
                for kc in range(KC):
                    P.op("ve", lambda v, kc=kc: v.tensor_copy(stg[:, :], xnT[:, kc, :]),
                         rd=["xnT"], wr=["dstg"])
                    P.op("sydma", lambda sy, kc=kc: sy.dma_start(
                        dbgt["xnT"].ap()[:, kc * TOK:(kc + 1) * TOK], stg[:, :]),
                        rd=["dstg"], wr=["dstg"])
                P.flush("dbg_xnT")

        with contextlib.ExitStack() as gst:
            GE_ = gst.enter_context
            Qf = GE_(nc.sbuf_tensor("Qf", [128, HPG, TOK], BF16))
            Kf = GE_(nc.sbuf_tensor("Kf", [128, HPG, TOK], BF16))
            Vt = GE_(nc.sbuf_tensor("Vt", [128, NCH, HPG * 128], BF16))
            og_buf = GE_(nc.sbuf_tensor("og_buf", [128, NCH, HPG * 128], BF16))
            ms_buf = GE_(nc.sbuf_tensor("ms_buf", [128, NCH * HPG], F32))
            wgb = [GE_(nc.sbuf_tensor(f"wgb{i}", [128, KC, HPG * 128], BF16))
                   for i in range(2)]

            def gw_of(j):
                idx = 0 if j == 0 else 1
                return wgb[idx], f"wgb{idx}"

            for j, wsrc in enumerate((wq_v, wk_v)):
                gw, gwn = gw_of(j)
                dma("gpdma", gw[:, :, :], wsrc[0], wr=[gwn])

            for g_i in range(NG):
                for h2 in range(HPG):
                    gh = g_i * HPG + h2
                    P.op("se", lambda s, h2=h2, gh=gh: s.activation(
                        Dt_g[:, h2 * 128:(h2 + 1) * 128], iota_fi[:, :], AF.Exp,
                        scale=ldb[:, gh:gh + 1]),
                        rd=["iota_fi", "ldb"], wr=[f"DtE{h2}"])
                    P.op("ge", lambda g, h2=h2: g.affine_select(
                        Dt_g[:, h2 * 128:(h2 + 1) * 128],
                        Dt_g[:, h2 * 128:(h2 + 1) * 128], [[1, 128]],
                        compare_op=ALU.is_ge, fill=0.0, base=0,
                        channel_multiplier=-1),
                        rd=[f"DtE{h2}"], wr=[f"Dt{h2}"])
                    P.op("se", lambda s, h2=h2, gh=gh: s.activation(
                        qd_g[:, h2 * 128:(h2 + 1) * 128], iota_f1[:, :], AF.Exp,
                        scale=ldb[:, gh:gh + 1]),
                        rd=["iota_f1", "ldb"], wr=[f"qd{h2}"])
                for wb_wn, dst, dn in ((gw_of(0), Qf, "Qf"),
                                       (gw_of(1), Kf, "Kf")):
                    wb, wn = wb_wn
                    for h2 in range(HPG):
                        for tt in range(4):
                            b = (h2 * 4 + tt) % 2
                            def mm_fn(t, wb=wb, h2=h2, tt=tt, b=b):
                                for kc in range(KC):
                                    inst = t.matmul(
                                        bank[b][:, 0:512],
                                        wb[:, kc, h2 * 128:(h2 + 1) * 128],
                                        xnT[:, kc, tt * 512:(tt + 1) * 512],
                                        start=(kc == 0), stop=(kc == KC - 1))
                                return inst
                            P.op("pe", mm_fn, rd=[wn, "xnT"], wr=[BN[b]])
                            P.op("se", lambda s, dst=dst, h2=h2, tt=tt, b=b:
                                 s.activation(dst[:, h2, tt * 512:(tt + 1) * 512],
                                              bank[b][:, 0:512], AF.Silu),
                                 rd=[BN[b]], wr=[dn])
                wvb, wvn = gw_of(2)
                dma("gpdma", wvb[:, :, :], wv_v[g_i], wr=[wvn])
                for tc in range(NCH):
                    b = tc % 2
                    def mm_fn(t, tc=tc, b=b, wvb=wvb):
                        for kc in range(KC):
                            inst = t.matmul(bank[b][:, 0:256],
                                            xnT[:, kc, tc * 128:(tc + 1) * 128],
                                            wvb[:, kc, :],
                                            start=(kc == 0), stop=(kc == KC - 1))
                        return inst
                    P.op("pe", mm_fn, rd=[wvn, "xnT"], wr=[BN[b]])
                    P.op("se", lambda s, tc=tc, b=b: s.activation(
                        Vt[:, tc, :], bank[b][:, 0:256], AF.Silu),
                        rd=[BN[b]], wr=[f"Vt{tc}"])
    
                if g_i + 1 < NG:
                    for j, wsrc in enumerate((wq_v, wk_v)):
                        gw, gwn = gw_of(j)
                        dma("gpdma", gw[:, :, :], wsrc[g_i + 1], wr=[gwn])

                for cch in range(NCH):
                    for h2 in range(HPG):
                        gh = g_i * HPG + h2
                        rb = (cch * HPG + h2) % 2
                        P.op("pe", lambda t, h2=h2, cch=cch: t.matmul(
                            bank[2][:, 0:128], Kf[:, h2, cch * 128:(cch + 1) * 128],
                            Qf[:, h2, cch * 128:(cch + 1) * 128],
                            start=True, stop=True),
                            rd=["Kf", "Qf"], wr=["b2"])
                        P.op("pe", lambda t, h2=h2, cch=cch: t.transpose(
                            tbank[0][:, 0:128],
                            Kf[:, h2, cch * 128:(cch + 1) * 128], idn_b[:, :]),
                            rd=["Kf", "idn_b"], wr=["tb0"])
                        P.op("ve", lambda v, h2=h2, rb=rb: v.tensor_tensor(
                            sdt_b[rb][:, :], bank[2][:, 0:128],
                            Dt_g[:, h2 * 128:(h2 + 1) * 128], ALU.mult),
                            rd=["b2", f"Dt{h2}"], wr=[f"sdt{rb}"])
                        P.op("ve", lambda v, gh=gh, rb=rb: v.tensor_scalar(
                            kscb[rb][:, :], tbank[0][:, 0:128], kdb[:, gh:gh + 1],
                            None, ALU.mult), rd=["tb0", "kdb"], wr=[f"ksc{rb}"])
                        P.op("ve", lambda v, h2=h2, cch=cch, rb=rb: v.tensor_tensor(
                            qs_b[rb][:, :], Qf[:, h2, cch * 128:(cch + 1) * 128],
                            qd_g[:, h2 * 128:(h2 + 1) * 128], ALU.mult),
                            rd=["Qf", f"qd{h2}"], wr=[f"qs{rb}"])
                        P.op("se", lambda s, gh=gh, rb=rb: s.copy(
                            sbf_b[rb][:, :], S_all[:, gh * 128:(gh + 1) * 128]),
                            rd=[f"S{gh}"], wr=[f"sbf{rb}"])
                        P.op("pe", lambda t, cch=cch, h2=h2, rb=rb: t.matmul(
                            bank[3][:, 0:128], kscb[rb][:, :],
                            Vt[:, cch, h2 * 128:(h2 + 1) * 128],
                            start=True, stop=True),
                            rd=[f"ksc{rb}", f"Vt{cch}"], wr=["b3"])
                        def o_fn(t, cch=cch, h2=h2, rb=rb):
                            t.matmul(bank[0][:, 0:128], sdt_b[rb][:, :],
                                     Vt[:, cch, h2 * 128:(h2 + 1) * 128],
                                     start=True, stop=False)
                            return t.matmul(bank[0][:, 0:128], qs_b[rb][:, :],
                                            sbf_b[rb][:, :], start=False, stop=True)
                        P.op("pe", o_fn,
                             rd=[f"sdt{rb}", f"Vt{cch}", f"qs{rb}", f"sbf{rb}"],
                             wr=["b0"])
                        P.op("pe", lambda t, cch=cch, gh=gh: t.matmul(
                            bank[1][:, 0:128], g1T[:, cch * 128:(cch + 1) * 128],
                            wg2_sb[:, gh * 128:(gh + 1) * 128],
                            start=True, stop=True),
                            rd=["g1T", "wg2"], wr=["b1"])
                        P.op("ve", lambda v, gh=gh: v.scalar_tensor_tensor(
                            S_all[:, gh * 128:(gh + 1) * 128],
                            S_all[:, gh * 128:(gh + 1) * 128],
                            cdb[:, gh:gh + 1], bank[3][:, 0:128],
                            ALU.mult, ALU.add),
                            rd=["b3", f"S{gh}", "cdb"], wr=[f"S{gh}"])
                        P.op("se", lambda s, rb=rb: s.activation(
                            sg_b[rb][:, :], bank[1][:, 0:128], AF.Sigmoid),
                            rd=["b1"], wr=[f"sg{rb}"])
                        P.op("ve", lambda v, cch=cch, h2=h2, rb=rb: v.tensor_tensor(
                            og_buf[:, cch, h2 * 128:(h2 + 1) * 128],
                            bank[0][:, 0:128], sg_b[rb][:, :], ALU.mult),
                            rd=["b0", f"sg{rb}"], wr=[f"og{cch}"])
                        P.op("se", lambda s, cch=cch, h2=h2: s.activation(
                            dmp[:, :], og_buf[:, cch, h2 * 128:(h2 + 1) * 128],
                            AF.Square,
                            accum_out=ms_buf[:, cch * HPG + h2:cch * HPG + h2 + 1]),
                            rd=[f"og{cch}"], wr=[f"ms{cch}", "dmp"])

                for cch in range(NCH):
                    h8 = cch // NHC
                    dsto = onorm[h8]
                    col = (cch % NHC) * 128
                    P.op("ve", lambda v, cch=cch: v.tensor_scalar(
                        sdsc[:, 0:2], ms_buf[:, cch * HPG:(cch + 1) * HPG],
                        1.0 / 128.0, EPS, ALU.mult, ALU.add),
                        rd=[f"ms{cch}"], wr=["sdscE"])
                    P.op("se", lambda s: s.activation(
                        sdsc[:, 0:2], sdsc[:, 0:2], AF.Sqrt),
                        rd=["sdscE"], wr=["sdsc"])
                    P.op("ve", lambda v: v.reciprocal(rstd_b[:, 0:2], sdsc[:, 0:2]),
                         rd=["sdsc"], wr=["rstdb"])
                    for h2 in range(HPG):
                        gh = g_i * HPG + h2
                        rb = (cch * HPG + h2) % 2
                        P.op("ve", lambda v, cch=cch, h2=h2, rb=rb: v.tensor_scalar(
                            on_b[rb][:, :], og_buf[:, cch, h2 * 128:(h2 + 1) * 128],
                            rstd_b[:, h2:h2 + 1], None, ALU.mult),
                            rd=[f"og{cch}", "rstdb"], wr=[f"on{rb}"])
                        P.op("pe", lambda t, rb=rb: t.transpose(
                            tbank[1][:, rb * 128:(rb + 1) * 128], on_b[rb][:, :],
                            idn_b[:, :]), rd=[f"on{rb}", "idn_b"], wr=["tb1"])
                        P.op("ve", lambda v, rb=rb, gh=gh, dsto=dsto, col=col, h8=h8:
                             v.tensor_scalar(dsto[:, gh, col:col + 128],
                                             tbank[1][:, rb * 128:(rb + 1) * 128],
                                             normw[:, gh:gh + 1], None, ALU.mult),
                             rd=["tb1", "normw"], wr=[f"on{h8}_{gh}"])
                if g_i == NG - 1:
                    P.flush("groups")

            if DBG:
                P.op("sydma", lambda sy: sy.dma_start(dbgt["S"].ap(), S_all[:, :]),
                     rd=SN, wr=["dbgsdma"])
                with contextlib.ExitStack() as ds:
                    stq = ds.enter_context(nc.sbuf_tensor("dbgstq", [128, 1024], F32))
                    for h2 in range(HPG):
                        for hv in range(2):
                            P.op("ve", lambda v, h2=h2, hv=hv: v.tensor_copy(
                                stq[:, :], Qf[:, h2, hv * 1024:(hv + 1) * 1024]),
                                rd=["Qf"], wr=["dstq"])
                            P.op("sydma", lambda sy, h2=h2, hv=hv: sy.dma_start(
                                dbgt["qf"].ap()[:, h2 * TOK + hv * 1024:
                                                h2 * TOK + (hv + 1) * 1024],
                                stq[:, :]), rd=["dstq"], wr=["dstq"])
                            P.op("ve", lambda v, h2=h2, hv=hv: v.tensor_copy(
                                stq[:, :], Kf[:, h2, hv * 1024:(hv + 1) * 1024]),
                                rd=["Kf"], wr=["dstq"])
                            P.op("sydma", lambda sy, h2=h2, hv=hv: sy.dma_start(
                                dbgt["kf"].ap()[:, h2 * TOK + hv * 1024:
                                                h2 * TOK + (hv + 1) * 1024],
                                stq[:, :]), rd=["dstq"], wr=["dstq"])
                    P.flush("dbg_qf")
                with contextlib.ExitStack() as ds:
                    stg = ds.enter_context(nc.sbuf_tensor("dbgstg2", [128, 256], F32))
                    for cch in range(NCH):
                        P.op("ve", lambda v, cch=cch: v.tensor_copy(
                            stg[:, cch * 128:(cch + 1) * 128]
                            if False else stg[:, 0:256], og_buf[:, cch, :]),
                            rd=[f"og{cch}"], wr=["dstg2"])
                        P.op("sydma", lambda sy, cch=cch: sy.dma_start(
                            dbgt["og"].ap()[:, cch * 256:(cch + 1) * 256],
                            stg[:, 0:256]), rd=["dstg2"], wr=["dstg2"])
                    P.flush("dbg_og")

        ln_st.close()
        at_st.close()

        for hf in range(2):
            onrm = onorm[hf]
            onn = [f"on{hf}_{gh}" for gh in range(NH)]
            YN = [f"y{hf}_{o2}" for o2 in range(KC)]
            with contextlib.ExitStack() as hst:
                HE = hst.enter_context
                yacc = HE(nc.sbuf_tensor(f"yacc{hf}", [128, KC, HALF], F32))
                xn2T = HE(nc.sbuf_tensor(f"xn2T{hf}", [128, KC, HALF], BF16))

                with contextlib.ExitStack() as wst:
                    WE = wst.enter_context
                    xbuf = [WE(nc.sbuf_tensor(f"h{hf}xb{i}", [128, D], F32))
                            for i in range(2)]
                    wog = WE(nc.sbuf_tensor(f"wog{hf}", [128, KC, 256], BF16))
                    for tc8 in range(NHC):
                        tc = hf * NHC + tc8
                        xb, xbn = xbuf[tc8 % 2], f"hxb{tc8 % 2}"
                        dma("sydma", xb[:, :], x_own_v[tc], wr=[xbn])
                        for q in range(4):
                            b = 4 + (q % 2)
                            def tr_fn(t, xb=xb, q=q, b=b):
                                for k4 in range(4):
                                    kc = q * 4 + k4
                                    inst = t.transpose(
                                        bank[b][:, k4 * 128:(k4 + 1) * 128],
                                        xb[:, kc * 128:(kc + 1) * 128], idn_f[:, :])
                                return inst
                            P.op("pe", tr_fn, rd=[xbn, "idn_f"], wr=[BN[b]])
                            def ev_fn(v, tc8=tc8, q=q, b=b):
                                for k4 in range(4):
                                    kc = q * 4 + k4
                                    inst = v.tensor_copy(
                                        yacc[:, kc, tc8 * 128:(tc8 + 1) * 128],
                                        bank[b][:, k4 * 128:(k4 + 1) * 128])
                                return inst
                            P.op("ve", ev_fn, rd=[BN[b]],
                                 wr=[YN[q * 4 + k4] for k4 in range(4)])

                    for sl in range(8):
                        dma("gpdma", wog[:, :, :], wo_v[sl], wr=["wog"])
                        for o2h in range(2):
                            o2 = sl * 2 + o2h
                            for tt in range(2):
                                b = (o2h * 2 + tt) % 2
                                def mm_fn(t, o2h=o2h, tt=tt, b=b):
                                    for kc in range(KC):
                                        inst = t.matmul(
                                            bank[b][:, 0:512],
                                            wog[:, kc, o2h * 128:(o2h + 1) * 128],
                                            onrm[:, kc, tt * 512:(tt + 1) * 512],
                                            start=(kc == 0), stop=(kc == KC - 1))
                                    return inst
                                P.op("pe", mm_fn, rd=["wog"] + onn, wr=[BN[b]])
                                P.op("ve", lambda v, o2=o2, tt=tt, b=b:
                                     v.tensor_tensor(
                                         yacc[:, o2, tt * 512:(tt + 1) * 512],
                                         yacc[:, o2, tt * 512:(tt + 1) * 512],
                                         bank[b][:, 0:512], ALU.add),
                                     rd=[BN[b], YN[o2]], wr=[YN[o2]])
                    P.flush(f"h{hf}_wo")

                if hf == 0:
                    on_st0.close()
                else:
                    on_st1.close()

                if DBG and hf == 0:
                    with contextlib.ExitStack() as ds:
                        stg = ds.enter_context(
                            nc.sbuf_tensor("dbgstg3", [128, HALF], F32))
                        for kc in range(KC):
                            P.op("ve", lambda v, kc=kc: v.tensor_copy(
                                stg[:, :], yacc[:, kc, :]), rd=[YN[kc]], wr=["ds3"])
                            P.op("sydma", lambda sy, kc=kc: sy.dma_start(
                                dbgt["yA"].ap()[:, kc * HALF:(kc + 1) * HALF],
                                stg[:, :]), rd=["ds3"], wr=["ds3"])
                        P.flush("dbg_yA")

                with contextlib.ExitStack() as lst2:
                    L2 = lst2.enter_context
                    ybf = [L2(nc.sbuf_tensor(f"h{hf}ybf{i}", [128, 512], BF16))
                           for i in range(2)]
                    r_m = L2(nc.sbuf_tensor(f"r_m{hf}", [1, 512], F32))
                    r_ms = L2(nc.sbuf_tensor(f"r_ms{hf}", [1, 512], F32))
                    r_t = L2(nc.sbuf_tensor(f"r_t{hf}", [1, 512], F32))
                    r_b = L2(nc.sbuf_tensor(f"r_b{hf}", [1, 512], F32))
                    ab = L2(nc.sbuf_tensor(f"ab{hf}", [128, 512], F32))
                    bb = L2(nc.sbuf_tensor(f"bb{hf}", [128, 512], F32))
                    t0b = L2(nc.sbuf_tensor(f"t0b{hf}", [128, 512], F32))
                    probs = L2(nc.sbuf_tensor(f"probs{hf}", [128, 16], F32))
                    ptmp = L2(nc.sbuf_tensor(f"ptmp{hf}", [128, 16], F32))
                    peq = L2(nc.sbuf_tensor(f"peq{hf}", [128, 16], F32))
                    pst = L2(nc.sbuf_tensor(f"pst{hf}", [128, 4], F32))
                    combT = L2(nc.sbuf_tensor(f"combT{hf}", [16, HALF], F32))

                    for tt in range(2):
                        for kc in range(KC):
                            rb = kc % 2
                            P.op("se", lambda s, kc=kc, tt=tt, rb=rb: s.copy(
                                ybf[rb][:, :], yacc[:, kc, tt * 512:(tt + 1) * 512]),
                                rd=[YN[kc]], wr=[f"ybf{rb}"])
                            P.op("pe", lambda t, kc=kc, rb=rb: t.matmul(
                                bank[5][0:1, 0:512], ones_b[:, 0:1], ybf[rb][:, :],
                                start=(kc == 0), stop=(kc == KC - 1)),
                                rd=[f"ybf{rb}", "ones_b"], wr=["b5"])
                        for kc in range(KC):
                            rb = kc % 2
                            P.op("se", lambda s, kc=kc, tt=tt, rb=rb: s.activation(
                                ybf[rb][:, :], yacc[:, kc, tt * 512:(tt + 1) * 512],
                                AF.Square), rd=[YN[kc]], wr=[f"ybf{rb}"])
                            P.op("pe", lambda t, kc=kc, rb=rb: t.matmul(
                                bank[4][0:1, 0:512], ones_b[:, 0:1], ybf[rb][:, :],
                                start=(kc == 0), stop=(kc == KC - 1)),
                                rd=[f"ybf{rb}", "ones_b"], wr=["b4"])
                        P.op("ve", lambda v: v.tensor_scalar(
                            r_m[0:1, :], bank[5][0:1, 0:512], 1.0 / D, None,
                            ALU.mult), rd=["b5"], wr=["r_m"])
                        P.op("ve", lambda v: v.tensor_scalar(
                            r_ms[0:1, :], bank[4][0:1, 0:512], 1.0 / D, None,
                            ALU.mult), rd=["b4"], wr=["r_ms"])
                        P.op("ve", lambda v: v.tensor_tensor(
                            r_t[0:1, :], r_m[0:1, :], r_m[0:1, :], ALU.mult),
                            rd=["r_m"], wr=["r_t"])
                        P.op("ve", lambda v: v.tensor_tensor(
                            r_ms[0:1, :], r_ms[0:1, :], r_t[0:1, :], ALU.subtract),
                            rd=["r_ms", "r_t"], wr=["r_ms"])
                        P.op("ve", lambda v: v.tensor_scalar(
                            r_ms[0:1, :], r_ms[0:1, :], EPS, None, ALU.add),
                            rd=["r_ms"], wr=["r_ms"])
                        P.op("se", lambda s: s.activation(
                            r_t[0:1, :], r_ms[0:1, :], AF.Sqrt),
                            rd=["r_ms"], wr=["r_t"])
                        P.op("ve", lambda v: v.reciprocal(r_ms[0:1, :], r_t[0:1, :]),
                             rd=["r_t"], wr=["r_ms"])
                        P.op("ve", lambda v: v.tensor_scalar(
                            r_t[0:1, :], r_ms[0:1, :], -1.0, None, ALU.mult),
                            rd=["r_ms"], wr=["r_t"])
                        P.op("ve", lambda v: v.tensor_tensor(
                            r_b[0:1, :], r_m[0:1, :], r_t[0:1, :], ALU.mult),
                            rd=["r_m", "r_t"], wr=["r_b"])
                        P.op("pe", lambda t: t.matmul(
                            bank[4][:, 0:512], ones[0:1, :], r_ms[0:1, :],
                            start=True, stop=True),
                            rd=["r_ms", "ones"], wr=["b4"])
                        P.op("ve", lambda v: v.tensor_copy(ab[:, :],
                                                           bank[4][:, 0:512]),
                             rd=["b4"], wr=["abf"])
                        P.op("pe", lambda t: t.matmul(
                            bank[4][:, 0:512], ones[0:1, :], r_b[0:1, :],
                            start=True, stop=True),
                            rd=["r_b", "ones"], wr=["b4"])
                        P.op("ve", lambda v: v.tensor_copy(bb[:, :],
                                                           bank[4][:, 0:512]),
                             rd=["b4"], wr=["bbf"])
                        for kc in range(KC):
                            P.op("ve", lambda v, kc=kc, tt=tt: v.tensor_tensor(
                                t0b[:, :], yacc[:, kc, tt * 512:(tt + 1) * 512],
                                ab[:, :], ALU.mult),
                                rd=[YN[kc], "abf"], wr=["t0b"])
                            P.op("ve", lambda v, kc=kc, tt=tt: v.tensor_tensor(
                                xn2T[:, kc, tt * 512:(tt + 1) * 512], t0b[:, :],
                                bb[:, :], ALU.add),
                                rd=["t0b", "bbf"], wr=["xn2T"])

                    if DBG and hf == 0:
                        with contextlib.ExitStack() as ds:
                            stg = ds.enter_context(
                                nc.sbuf_tensor("dbgstg4", [128, HALF], F32))
                            for kc in range(KC):
                                P.op("ve", lambda v, kc=kc: v.tensor_copy(
                                    stg[:, :], xn2T[:, kc, :]),
                                    rd=["xn2T"], wr=["ds4"])
                                P.op("sydma", lambda sy, kc=kc: sy.dma_start(
                                    dbgt["xn2A"].ap()[:, kc * HALF:(kc + 1) * HALF],
                                    stg[:, :]), rd=["ds4"], wr=["ds4"])
                            P.flush("dbg_xn2A")

                    for tc in range(NHC):
                        def lg_fn(t, tc=tc):
                            for kc in range(KC):
                                inst = t.matmul(bank[5][:, 0:16],
                                                xn2T[:, kc, tc * 128:(tc + 1) * 128],
                                                gatewT[:, kc, :],
                                                start=(kc == 0), stop=(kc == KC - 1))
                            return inst
                        P.op("pe", lg_fn, rd=["xn2T", "gatewT"], wr=["b5"])
                        P.op("ve", lambda v: v.reduce_max(
                            pst[:, 0:1], bank[5][:, 0:16], axis=AX.X, negate=True),
                            rd=["b5"], wr=["pst"])
                        P.op("se", lambda s: s.activation(
                            probs[:, :], bank[5][:, 0:16], AF.Exp,
                            bias=pst[:, 0:1], accum_out=pst[:, 1:2]),
                            rd=["b5", "pst"], wr=["probs", "pst"])
                        P.op("ve", lambda v: v.reciprocal(pst[:, 2:3], pst[:, 1:2]),
                             rd=["pst"], wr=["pst"])
                        P.op("ve", lambda v: v.tensor_scalar(
                            probs[:, :], probs[:, :], pst[:, 2:3], None, ALU.mult),
                            rd=["probs", "pst"], wr=["probs"])
                        P.op("ve", lambda v: v.tensor_copy(ptmp[:, :], probs[:, :]),
                             rd=["probs"], wr=["ptmp"])
                        for r in range(5):
                            P.op("ve", lambda v: v.reduce_max(
                                pst[:, 3:4], ptmp[:, :], axis=AX.X),
                                rd=["ptmp"], wr=["pst"])
                            P.op("ve", lambda v: v.tensor_scalar(
                                peq[:, :], ptmp[:, :], pst[:, 3:4], None, ALU.is_ge),
                                rd=["ptmp", "pst"], wr=["peq"])
                            P.op("ve", lambda v: v.scalar_tensor_tensor(
                                ptmp[:, :], peq[:, :], -1e9, ptmp[:, :],
                                ALU.mult, ALU.add),
                                rd=["peq", "ptmp"], wr=["ptmp"])
                        P.op("ve", lambda v: v.reduce_max(
                            pst[:, 3:4], ptmp[:, :], axis=AX.X),
                            rd=["ptmp"], wr=["pst"])
                        P.op("ve", lambda v: v.tensor_scalar(
                            peq[:, :], probs[:, :], pst[:, 3:4], None, ALU.is_ge),
                            rd=["probs", "pst"], wr=["peq"])
                        P.op("ve", lambda v: v.tensor_tensor(
                            probs[:, :], probs[:, :], peq[:, :], ALU.mult),
                            rd=["probs", "peq"], wr=["probs"])
                        P.op("ve", lambda v: v.reduce_sum(
                            pst[:, 3:4], probs[:, :], axis=AX.X),
                            rd=["probs"], wr=["pst"])
                        P.op("ve", lambda v: v.tensor_scalar(
                            pst[:, 3:4], pst[:, 3:4], 1e-20, None, ALU.add),
                            rd=["pst"], wr=["pst"])
                        P.op("ve", lambda v: v.reciprocal(pst[:, 2:3], pst[:, 3:4]),
                             rd=["pst"], wr=["pst"])
                        P.op("ve", lambda v: v.tensor_scalar(
                            probs[:, :], probs[:, :], pst[:, 2:3], None, ALU.mult),
                            rd=["probs", "pst"], wr=["probs"])
                        P.op("pe", lambda t: t.transpose(
                            bank[4][0:16, 0:128], probs[:, :], idn_f[:, :]),
                            rd=["probs", "idn_f"], wr=["b4"])
                        P.op("ve", lambda v, tc=tc: v.tensor_copy(
                            combT[:, tc * 128:(tc + 1) * 128], bank[4][0:16, 0:128]),
                            rd=["b4"], wr=["combT"])
                    dma("sydma", comb_dram.ap(), combT[:, :], rd=["combT"],
                        wr=["combdram"])
                    if DBG and hf == 0:
                        P.op("sydma", lambda sy: sy.dma_start(
                            dbgt["cA"].ap(), combT[:, :]), rd=["combT"],
                            wr=["dbgcdma"])
                    P.flush(f"h{hf}_gate")

                with contextlib.ExitStack() as fst:
                    FE = fst.enter_context
                    crow = FE(nc.sbuf_tensor(f"crow{hf}", [1, HALF], F32))
                    cbc = FE(nc.sbuf_tensor(f"cbc{hf}", [128, HALF], F32))
                    ws13 = [FE(nc.sbuf_tensor(f"h{hf}ws13_{i}", [128, KC, 256], BF16))
                            for i in range(2)]
                    ws2 = [FE(nc.sbuf_tensor(f"h{hf}ws2_{i}", [128, ICH, 256], BF16))
                           for i in range(2)]
                    hbuf = FE(nc.sbuf_tensor(f"hbuf{hf}", [128, ICH, HALF], BF16))
                    sg2 = [FE(nc.sbuf_tensor(f"h{hf}sg2_{i}", [128, 512], BF16))
                           for i in range(2)]
                    su2 = [FE(nc.sbuf_tensor(f"h{hf}su2_{i}", [128, 512], BF16))
                           for i in range(2)]
                    cd_flat = comb_dram.ap().rearrange("e t -> (e t)")

                    for e in range(NE):
                        if e < 16:
                            P.op("sydma", lambda sy, e=e: sy.dma_start(
                                crow[0:1, :], cd_flat[None, e * HALF:(e + 1) * HALF]),
                                rd=["combdram"], wr=["crow"])
                            for tt in range(2):
                                P.op("pe", lambda t, tt=tt: t.matmul(
                                    bank[4][:, 0:512], ones[0:1, :],
                                    crow[0:1, tt * 512:(tt + 1) * 512],
                                    start=True, stop=True),
                                    rd=["crow", "ones"], wr=["b4"])
                                P.op("ve", lambda v, tt=tt: v.tensor_copy(
                                    cbc[:, tt * 512:(tt + 1) * 512],
                                    bank[4][:, 0:512]), rd=["b4"], wr=["cbc"])
                        for j in range(ICH):
                            wb, wbn = ws13[j % 2], f"ws13_{j % 2}"
                            dma("gpdma", wb[:, :, :], w13_v[e, j], wr=[wbn])
                            for tt in range(2):
                                rb = tt
                                def g_fn(t, wb=wb, tt=tt):
                                    for kc in range(KC):
                                        inst = t.matmul(
                                            bank[0][:, 0:512], wb[:, kc, 0:128],
                                            xn2T[:, kc, tt * 512:(tt + 1) * 512],
                                            start=(kc == 0), stop=(kc == KC - 1))
                                    return inst
                                P.op("pe", g_fn, rd=[wbn, "xn2T"], wr=["b0"])
                                P.op("se", lambda s, rb=rb: s.activation(
                                    sg2[rb][:, :], bank[0][:, 0:512], AF.Silu),
                                    rd=["b0"], wr=[f"hsg{rb}"])
                                def u_fn(t, wb=wb, tt=tt):
                                    for kc in range(KC):
                                        inst = t.matmul(
                                            bank[1][:, 0:512], wb[:, kc, 128:256],
                                            xn2T[:, kc, tt * 512:(tt + 1) * 512],
                                            start=(kc == 0), stop=(kc == KC - 1))
                                    return inst
                                P.op("pe", u_fn, rd=[wbn, "xn2T"], wr=["b1"])
                                if e < 16:
                                    P.op("ve", lambda v, rb=rb, tt=tt: v.tensor_tensor(
                                        su2[rb][:, :], bank[1][:, 0:512],
                                        cbc[:, tt * 512:(tt + 1) * 512], ALU.mult),
                                        rd=["b1", "cbc"], wr=[f"hsu{rb}"])
                                else:
                                    P.op("ve", lambda v, rb=rb: v.tensor_copy(
                                        su2[rb][:, :], bank[1][:, 0:512]),
                                        rd=["b1"], wr=[f"hsu{rb}"])
                                P.op("ve", lambda v, j=j, tt=tt, rb=rb: v.tensor_tensor(
                                    hbuf[:, j, tt * 512:(tt + 1) * 512],
                                    sg2[rb][:, :], su2[rb][:, :], ALU.mult),
                                    rd=[f"hsg{rb}", f"hsu{rb}"], wr=[f"hb{j}"])
                        for sl in range(8):
                            wb, wbn = ws2[sl % 2], f"ws2_{sl % 2}"
                            dma("gpdma", wb[:, :, :], w2_v[e, sl], wr=[wbn])
                            for o2h in range(2):
                                o2 = sl * 2 + o2h
                                for tt in range(2):
                                    b = (o2h * 2 + tt) % 2
                                    def y_fn(t, wb=wb, o2h=o2h, tt=tt, b=b):
                                        for ic in range(ICH):
                                            inst = t.matmul(
                                                bank[b][:, 0:512],
                                                wb[:, ic, o2h * 128:(o2h + 1) * 128],
                                                hbuf[:, ic, tt * 512:(tt + 1) * 512],
                                                start=(ic == 0), stop=(ic == ICH - 1))
                                        return inst
                                    P.op("pe", y_fn,
                                         rd=[wbn] + [f"hb{ic}" for ic in range(ICH)],
                                         wr=[BN[b]])
                                    P.op("ve", lambda v, o2=o2, tt=tt, b=b:
                                         v.tensor_tensor(
                                             yacc[:, o2, tt * 512:(tt + 1) * 512],
                                             yacc[:, o2, tt * 512:(tt + 1) * 512],
                                             bank[b][:, 0:512], ALU.add),
                                         rd=[BN[b], YN[o2]], wr=[YN[o2]])

                    P.flush(f"h{hf}_ffn")

                with contextlib.ExitStack() as ost:
                    OE = ost.enter_context
                    outst = [OE(nc.sbuf_tensor(f"h{hf}outst{i}", [128, D], F32))
                             for i in range(2)]
                    for tc8 in range(NHC):
                        tc = hf * NHC + tc8
                        ob, obn = outst[tc8 % 2], f"outst{tc8 % 2}"
                        for q in range(4):
                            b = 4 + (q % 2)
                            def tr_fn(t, q=q, b=b, tc8=tc8):
                                for k4 in range(4):
                                    kc = q * 4 + k4
                                    inst = t.transpose(
                                        bank[b][:, k4 * 128:(k4 + 1) * 128],
                                        yacc[:, kc, tc8 * 128:(tc8 + 1) * 128],
                                        idn_f[:, :])
                                return inst
                            P.op("pe", tr_fn,
                                 rd=[YN[q * 4 + k4] for k4 in range(4)] + ["idn_f"],
                                 wr=[BN[b]])
                            def ev_fn(v, q=q, b=b, ob=ob):
                                for k4 in range(4):
                                    kc = q * 4 + k4
                                    inst = v.tensor_copy(
                                        ob[:, kc * 128:(kc + 1) * 128],
                                        bank[b][:, k4 * 128:(k4 + 1) * 128])
                                return inst
                            P.op("ve", ev_fn, rd=[BN[b]], wr=[obn])
                        dma("sydma", out_v[tc], ob[:, :], rd=[obn])
                    P.flush(f"h{hf}_out")

        fins = [(P.sems[f"{q}{i}"], P.cnt[f"{q}{i}"])
                for q in ("dw", "dx") for i in range(8)]
        with nc.Block("fin") as blk:
            @blk.sync
            def _(sy):
                for s, t in fins:
                    if t:
                        sy.wait_ge(s, t)
    return nc



def _pack_cols(w, nslice, width):
    din, dout = w.shape
    kc = din // 128
    r = w.reshape(kc, 128, nslice, width).transpose(2, 1, 0, 3)
    return np.ascontiguousarray(r.reshape(nslice, 128, kc * width))


def _layouts(inputs):
    f = np.ascontiguousarray
    x = np.asarray(inputs["L0_attn_x"], dtype=np.float32)
    ld = f(np.asarray(inputs["L0_attn_log_decay"], np.float32).reshape(1, NH))
    wq = np.asarray(inputs["L0_attn_Wq"], np.float32)
    wk = np.asarray(inputs["L0_attn_Wk"], np.float32)
    wv = np.asarray(inputs["L0_attn_Wv"], np.float32)
    wo = np.asarray(inputs["L0_attn_Wo"], np.float32)
    wg1 = np.asarray(inputs["L0_attn_Wg1"], np.float32)
    wg2 = f(np.asarray(inputs["L0_attn_Wg2"], np.float32))
    normw = f(np.asarray(inputs["L0_attn_norm_w"], np.float32).reshape(KC, 128).T)
    gatew = np.asarray(inputs["L0_ffn_gate_w"], np.float32).T
    w13 = np.asarray(inputs["L0_ffn_w13"], np.float32)
    w2 = np.asarray(inputs["L0_ffn_w2"], np.float32)
    sw13 = np.asarray(inputs["L0_ffn_sw13"], np.float32)
    sw2 = np.asarray(inputs["L0_ffn_sw2"], np.float32)

    wq8 = _pack_cols(wq, NG, 256)
    wk8 = _pack_cols(wk, NG, 256)
    wv8 = _pack_cols(wv, NG, 256)
    wo8 = _pack_cols(wo, NG, 256)
    wg1p = f(wg1.reshape(KC, 128, 128).transpose(1, 0, 2).reshape(128, KC * 128))
    gatewp = f(gatew.reshape(KC, 128, 16).transpose(1, 0, 2).reshape(128, KC * 16))

    w13T = w13.transpose(0, 2, 1)
    w13T = w13T.reshape(16, D, 2, ICH, 128).transpose(0, 1, 3, 2, 4).reshape(16, D, 2816)
    sw13T = sw13.T.reshape(D, 2, 22, 128).transpose(0, 2, 1, 3).reshape(D, 5632)
    w13all = np.concatenate([w13T, sw13T[None, :, 0:2816],
                             sw13T[None, :, 2816:5632]], axis=0)
    w13x = w13all.reshape(NE, KC, 128, ICH, 256).transpose(0, 3, 2, 1, 4)
    w13x = f(w13x.reshape(NE, ICH, 128, KC * 256))

    w2T = w2.transpose(0, 2, 1)
    sw2T = sw2.T
    w2all = np.concatenate([w2T, sw2T[None, 0:1408, :], sw2T[None, 1408:2816, :]],
                           axis=0)
    w2x = w2all.reshape(NE, ICH, 128, 8, 256).transpose(0, 3, 2, 1, 4)
    w2x = f(w2x.reshape(NE, 8, 128, ICH * 256))

    return dict(x=x, ld=ld, wq8=wq8, wk8=wk8, wv8=wv8, wo8=wo8, wg1p=wg1p,
                wg2p=wg2, normw=normw, gatewp=gatewp, w13x=w13x, w2x=w2x)


_NC_CACHE = {}
_LAST_RES = None


def kernel(**inputs):
    L = _layouts(inputs)
    x = L["x"].reshape(4, 2, TOK, D)
    flag1 = np.ones((128, 1), np.float32)
    flag0 = np.zeros((128, 1), np.float32)
    in_maps = []
    for core in range(8):
        b, hf = core // 2, core % 2
        in_maps.append(dict(
            x_own=np.ascontiguousarray(x[b, hf]),
            x_peer=np.ascontiguousarray(x[b, 0]),
            flag=(flag1 if hf == 1 else flag0),
            ld=L["ld"], wq8=L["wq8"], wk8=L["wk8"], wv8=L["wv8"], wo8=L["wo8"],
            wg1p=L["wg1p"], wg2p=L["wg2p"], normw=L["normw"], gatewp=L["gatewp"],
            w13x=L["w13x"], w2x=L["w2x"]))
    if "nc" not in _NC_CACHE:
        _NC_CACHE["nc"] = build_nc()
    nc = _NC_CACHE["nc"]
    res = run_bass_kernel_spmd(nc, in_maps, core_ids=list(range(8)))
    global _LAST_RES
    _LAST_RES = res
    out = np.empty((4, 2, TOK, D), np.float32)
    for core in range(8):
        b, hf = core // 2, core % 2
        out[b, hf] = res.results[core]["out"]
    return out.reshape(4, 4096, D)


# version 05
# speedup vs baseline: 1.0440x; 1.0257x over previous
import contextlib
import numpy as np
import ml_dtypes
import concourse.bass as bass
import concourse.mybir as mybir
from concourse.bass_utils import run_bass_kernel_spmd

F32 = mybir.dt.float32
BF16 = mybir.dt.bfloat16
AF = mybir.ActivationFunctionType
ALU = mybir.AluOpType
AX = mybir.AxisListType

D = 2048
KC = 16
TOK = 2048
NCH = 16
HALF = 1024
NHC = 8
NH = 16
HPG = 2
NG = NH // HPG
NE = 18
ICH = 11
EPS = 1e-5

DBG = False

ENG_SEM = {"pe": "pe", "ve": "ve", "se": "se", "ge": "ge",
           "sydma": "dx", "gpdma": "dw"}
ENG_PHYS = {"pe": "tensor", "ve": "vector", "se": "scalar", "ge": "gpsimd",
            "sydma": "sync", "gpdma": "gpsimd"}


class Prog:
    def __init__(self, nc):
        self.nc = nc
        self.sems = {}
        self.cnt = {}
        self.items = []
        self.last_w = {}
        self.readers = {}
        self.waited = {}

    def init_sems(self, stack):
        for s in ["pe", "ve", "se", "ge"]:
            self.sems[s] = stack.enter_context(self.nc.semaphore(s))
            self.cnt[s] = 0
        for q in ["dx", "dw"]:
            for i in range(8):
                s = f"{q}{i}"
                self.sems[s] = stack.enter_context(self.nc.semaphore(s))
                self.cnt[s] = 0
        self.dma_rr = {"dx": 0, "dw": 0}

    def op(self, eng, fn, rd=(), wr=(), deps=()):
        sem = ENG_SEM[eng]
        phys = ENG_PHYS[eng]
        if sem in ("dx", "dw"):
            inc = 16
            i = self.dma_rr[sem]
            self.dma_rr[sem] = (i + 1) % 8
            sem = f"{sem}{i}"
        else:
            inc = 1
        need = {}

        def add(d):
            if d is None:
                return
            s, t = d
            if need.get(s, -1) < t:
                need[s] = t

        for r in rd:
            add(self.last_w.get(r))
        for r in wr:
            add(self.last_w.get(r))
            for s, t in self.readers.get(r, {}).items():
                add((s, t))
        for d in deps:
            add(d)
        w = self.waited.setdefault(phys, {})
        waits = {}
        for s, t in sorted(need.items()):
            if w.get(s, -1) >= t:
                continue
            waits[s] = t
            w[s] = t
        self.cnt[sem] += inc
        tgt = self.cnt[sem]
        label = (sem, tgt)
        for r in rd:
            self.readers.setdefault(r, {})
            if self.readers[r].get(sem, -1) < tgt:
                self.readers[r][sem] = tgt
        for r in wr:
            self.last_w[r] = label
            self.readers[r] = {}
        self.items.append((phys, fn, waits, sem, inc))
        return label

    def flush(self, name):
        if not self.items:
            return
        items = self.items
        self.items = []
        by_eng = {}
        for it in items:
            by_eng.setdefault(it[0], []).append(it)

        def mk_body(eng_items):
            def body(eng):
                for _, fn, waits, sem, inc in eng_items:
                    for s, t in waits.items():
                        eng.wait_ge(self.sems[s], t)
                    inst = fn(eng)
                    inst.then_inc(self.sems[sem], inc)
            return body

        with self.nc.Block(name) as blk:
            for phys, eng_items in by_eng.items():
                getattr(blk, phys)(mk_body(eng_items))


def build_nc():
    nc = bass.Bass("TRN2", target_bir_lowering=False)

    x_own = nc.dram_tensor("x_own", [TOK, D], F32, kind="ExternalInput")
    x_peer = nc.dram_tensor("x_peer", [TOK, D], F32, kind="ExternalInput")
    flag_in = nc.dram_tensor("flag", [128, 1], F32, kind="ExternalInput")
    ld_in = nc.dram_tensor("ld", [1, NH], F32, kind="ExternalInput")
    wq_in = nc.dram_tensor("wq8", [NG, 128, KC * 256], F32, kind="ExternalInput")
    wk_in = nc.dram_tensor("wk8", [NG, 128, KC * 256], F32, kind="ExternalInput")
    wv_in = nc.dram_tensor("wv8", [NG, 128, KC * 256], F32, kind="ExternalInput")
    wo_in = nc.dram_tensor("wo8", [NG, 128, KC * 256], F32, kind="ExternalInput")
    wg1_in = nc.dram_tensor("wg1p", [128, KC * 128], F32, kind="ExternalInput")
    wg2_in = nc.dram_tensor("wg2p", [128, D], F32, kind="ExternalInput")
    normw_in = nc.dram_tensor("normw", [128, KC], F32, kind="ExternalInput")
    gatew_in = nc.dram_tensor("gatewp", [128, KC * 16], F32, kind="ExternalInput")
    w13_in = nc.dram_tensor("w13x", [NE, ICH, 128, KC * 256], F32, kind="ExternalInput")
    w2_in = nc.dram_tensor("w2x", [NE, 8, 128, ICH * 256], F32, kind="ExternalInput")
    out_t = nc.dram_tensor("out", [TOK, D], F32, kind="ExternalOutput")
    comb_dram = nc.dram_tensor("comb_scr", [16, HALF], F32, kind="Internal")

    dbgt = {}
    if DBG:
        for nm, shp in [("xnT", [128, KC * TOK]), ("S", [128, NH * 128]),
                        ("qf", [128, HPG * TOK]), ("kf", [128, HPG * TOK]),
                        ("onA", [128, KC * HALF]), ("yA", [128, KC * HALF]),
                        ("xn2A", [128, KC * HALF]), ("cA", [16, HALF]),
                        ("og", [128, NCH * HPG * 128])]:
            dbgt[nm] = nc.dram_tensor("dbg_" + nm, shp, F32, kind="ExternalOutput")

    x_own_v = x_own.ap().rearrange("(tc p) d -> tc p d", p=128)
    x_peer_v = x_peer.ap().rearrange("(tc p) d -> tc p d", p=128)
    out_v = out_t.ap().rearrange("(tc p) d -> tc p d", p=128)
    wq_v, wk_v, wv_v, wo_v = wq_in.ap(), wk_in.ap(), wv_in.ap(), wo_in.ap()
    w13_v, w2_v = w13_in.ap(), w2_in.ap()

    P = Prog(nc)
    st = contextlib.ExitStack()
    with st:
        E = st.enter_context
        P.init_sems(st)

        bank = [E(nc.psum_tensor(f"pb{i}", [128, 512], F32)) for i in range(6)]
        tbank = [E(nc.psum_tensor(f"pt{i}", [128, 1024], BF16)) for i in range(2)]
        BN = [f"b{i}" for i in range(6)]
        TBN = ["tb0", "tb1"]

        def dma(eng, dst, src, rd=(), wr=(), deps=()):
            return P.op(eng, lambda g, dst=dst, src=src: g.dma_start(dst, src),
                        rd=rd, wr=wr, deps=deps)

        R = dict(side="right")
        ones = E(nc.sbuf_tensor("ones", [128, 128], F32, **R))
        ones_b = E(nc.sbuf_tensor("ones_b", [128, 128], BF16, **R))
        idn_f = E(nc.sbuf_tensor("idn_f", [128, 128], F32, **R))
        idn_b = E(nc.sbuf_tensor("idn_b", [128, 128], BF16, **R))
        iota_fi = E(nc.sbuf_tensor("iota_fi", [128, 128], F32, **R))
        iota_f1 = E(nc.sbuf_tensor("iota_f1", [128, 128], F32, **R))
        iota_kd = E(nc.sbuf_tensor("iota_kd", [128, 1], F32, **R))
        ld_sb = E(nc.sbuf_tensor("ld_sb", [1, NH], F32, **R))
        ldb = E(nc.sbuf_tensor("ldb", [128, NH], F32, **R))
        cdb = E(nc.sbuf_tensor("cdb", [128, NH], F32, **R))
        kdb = E(nc.sbuf_tensor("kdb", [128, NH], F32, **R))
        flag = E(nc.sbuf_tensor("flag_sb", [128, 1], F32, **R))
        normw = E(nc.sbuf_tensor("normw_sb", [128, KC], F32, **R))
        gatewT = E(nc.sbuf_tensor("gatewT", [128, KC, 16], BF16, **R))
        dmp = E(nc.sbuf_tensor("dmp", [128, 128], BF16, **R))
        on_st1 = contextlib.ExitStack()
        onorm1 = on_st1.enter_context(nc.sbuf_tensor("onorm1", [128, KC, HALF], BF16, **R))
        on_st0 = contextlib.ExitStack()
        onorm0 = on_st0.enter_context(nc.sbuf_tensor("onorm0", [128, KC, HALF], BF16, **R))
        onorm = [onorm0, onorm1]

        at_st = contextlib.ExitStack()
        AE = at_st.enter_context
        wg2_sb = AE(nc.sbuf_tensor("wg2_sb", [128, D], BF16, **R))
        S_all = AE(nc.sbuf_tensor("S_all", [128, NH * 128], F32, **R))
        Dt_g = AE(nc.sbuf_tensor("Dt_g", [128, HPG * 128], F32, **R))
        qd_g = AE(nc.sbuf_tensor("qd_g", [128, HPG * 128], F32, **R))
        sdt_b = [AE(nc.sbuf_tensor(f"sdt{i}", [128, 128], BF16, **R)) for i in range(2)]
        kscb = [AE(nc.sbuf_tensor(f"kscb{i}", [128, 128], BF16, **R)) for i in range(2)]
        qs_b = [AE(nc.sbuf_tensor(f"qsb{i}", [128, 128], BF16, **R)) for i in range(2)]
        sbf_b = [AE(nc.sbuf_tensor(f"sbfb{i}", [128, 128], BF16, **R)) for i in range(2)]
        sg_b = [AE(nc.sbuf_tensor(f"sgb{i}", [128, 128], F32, **R)) for i in range(2)]
        on_b = [AE(nc.sbuf_tensor(f"onb{i}", [128, 128], BF16, **R)) for i in range(2)]
        rstd_b = AE(nc.sbuf_tensor("rstd_b", [128, 2], F32, **R))
        sdsc = AE(nc.sbuf_tensor("sdsc", [128, 2], F32, **R))
        wg1_st = contextlib.ExitStack()
        wg1_sb = wg1_st.enter_context(nc.sbuf_tensor("wg1_sb", [128, KC, 128], BF16, **R))

        SN = [f"S{h}" for h in range(NH)]

        dma("gpdma", ld_sb[:, :], ld_in.ap(), wr=["ld_sb"])
        dma("gpdma", flag[:, :], flag_in.ap(), wr=["flag"])
        dma("gpdma", normw[:, :], normw_in.ap(), wr=["normw"])
        dma("gpdma", gatewT[:, :, :], gatew_in.ap(), wr=["gatewT"])
        dma("gpdma", wg1_sb[:, :, :], wg1_in.ap(), wr=["wg1"])
        dma("gpdma", wg2_sb[:, :], wg2_in.ap(), wr=["wg2"])
        P.op("ge", lambda g: g.iota(iota_fi[:, :], [[1, 128]], base=0,
                                    channel_multiplier=-1,
                                    allow_small_or_imprecise_dtypes=True),
             wr=["iota_fi"])
        P.op("ge", lambda g: g.iota(iota_f1[:, :], [[1, 128]], base=1,
                                    channel_multiplier=0,
                                    allow_small_or_imprecise_dtypes=True),
             wr=["iota_f1"])
        P.op("ge", lambda g: g.iota(iota_kd[:, :], [[0, 1]], base=127,
                                    channel_multiplier=-1,
                                    allow_small_or_imprecise_dtypes=True),
             wr=["iota_kd"])
        P.op("ge", lambda g: g.memset(ones[:, :], 1.0), wr=["ones"])
        P.op("ge", lambda g: g.memset(S_all[:, :], 0.0), wr=SN)
        P.op("ge", lambda g: g.affine_select(idn_f[:, :], ones[:, :], [[1, 128]],
                                             compare_op=ALU.is_equal, fill=0.0,
                                             base=0, channel_multiplier=-1),
             rd=["ones"], wr=["idn_f"])
        P.op("ve", lambda v: v.tensor_copy(idn_b[:, :], idn_f[:, :]),
             rd=["idn_f"], wr=["idn_b"])
        P.op("ve", lambda v: v.tensor_copy(ones_b[:, :], ones[:, :]),
             rd=["ones"], wr=["ones_b"])

        def ldbcast_fn(t):
            for h in range(NH):
                inst = t.matmul(bank[5][:, h:h + 1], ones[0:1, :],
                                ld_sb[0:1, h:h + 1], start=True, stop=True)
            return inst
        P.op("pe", ldbcast_fn, rd=["ones", "ld_sb"], wr=["b5"])
        P.op("ve", lambda v: v.tensor_copy(ldb[:, :], bank[5][:, 0:NH]),
             rd=["b5"], wr=["ldb"])
        P.op("ve", lambda v: v.tensor_scalar(cdb[:, :], ldb[:, :], 128.0, None,
                                             ALU.mult), rd=["ldb"], wr=["cdbE"])
        P.op("se", lambda s: s.activation(cdb[:, :], cdb[:, :], AF.Exp),
             rd=["cdbE"], wr=["cdb"])

        def kdb_fn(s):
            for h in range(NH):
                inst = s.activation(kdb[:, h:h + 1], iota_kd[:, :], AF.Exp,
                                    scale=ldb[:, h:h + 1])
            return inst
        P.op("se", kdb_fn, rd=["ldb", "iota_kd"], wr=["kdb"])
        P.flush("setup")

        def emit_ln1(x_view, tc_lo, tc_hi, xnT, xname, bufs):
            xbuf, xn_tok, dump, lnst = bufs
            nxb = len(xbuf)
            for tc in range(tc_lo, tc_hi):
                i = tc - tc_lo
                xb, xbn = xbuf[i % nxb], f"xb{i % nxb}"
                ls, lsn = lnst[i % 2], f"lnst{i % 2}"
                xn, xnn = xn_tok[i % 2], f"xntok{i % 2}"
                dma("sydma", xb[:, :], x_view[tc], wr=[xbn])
                P.op("ve", lambda v, xb=xb, ls=ls: v.reduce_sum(
                    ls[:, 0:1], xb[:, :], axis=AX.X), rd=[xbn], wr=[lsn])
                P.op("se", lambda s, xb=xb, ls=ls: s.activation(
                    dump[:, :], xb[:, :], AF.Square, accum_out=ls[:, 1:2]),
                    rd=[xbn, lsn], wr=[lsn, "lndump"])
                def s_a(v, ls=ls):
                    v.tensor_scalar(ls[:, 2:3], ls[:, 0:1], 1.0 / D, None, ALU.mult)
                    return v.tensor_scalar(ls[:, 3:4], ls[:, 1:2], 1.0 / D, None,
                                           ALU.mult)
                P.op("ve", s_a, rd=[lsn], wr=[lsn])
                P.op("ve", lambda v, ls=ls: v.tensor_tensor(
                    ls[:, 4:5], ls[:, 2:3], ls[:, 2:3], ALU.mult), rd=[lsn], wr=[lsn])
                P.op("ve", lambda v, ls=ls: v.tensor_tensor(
                    ls[:, 5:6], ls[:, 3:4], ls[:, 4:5], ALU.subtract),
                    rd=[lsn], wr=[lsn])
                P.op("ve", lambda v, ls=ls: v.tensor_scalar(
                    ls[:, 5:6], ls[:, 5:6], EPS, None, ALU.add), rd=[lsn], wr=[lsn])
                P.op("se", lambda s, ls=ls: s.activation(
                    ls[:, 6:7], ls[:, 5:6], AF.Sqrt), rd=[lsn], wr=[lsn])
                P.op("ve", lambda v, ls=ls: v.reciprocal(ls[:, 7:8], ls[:, 6:7]),
                     rd=[lsn], wr=[lsn])
                P.op("ve", lambda v, xb=xb, ls=ls, xn=xn: v.tensor_scalar(
                    xn[:, :], xb[:, :], ls[:, 2:3], ls[:, 7:8],
                    ALU.subtract, ALU.mult), rd=[xbn, lsn], wr=[xnn])
                for h8 in range(2):
                    tb, tbn = tbank[h8], TBN[h8]
                    def tr_fn(t, xn=xn, h8=h8, tb=tb):
                        for k8 in range(8):
                            kc = h8 * 8 + k8
                            inst = t.transpose(tb[:, k8 * 128:(k8 + 1) * 128],
                                               xn[:, kc * 128:(kc + 1) * 128],
                                               idn_b[:, :])
                        return inst
                    P.op("pe", tr_fn, rd=[xnn, "idn_b"], wr=[tbn])
                    def ev_fn(v, i=i, h8=h8, tb=tb, xnT=xnT):
                        for k8 in range(8):
                            kc = h8 * 8 + k8
                            inst = v.tensor_copy(
                                xnT[:, kc, i * 128:(i + 1) * 128],
                                tb[:, k8 * 128:(k8 + 1) * 128])
                        return inst
                    P.op("ve", ev_fn, rd=[tbn], wr=[xname])

        with contextlib.ExitStack() as rst:
            RE = rst.enter_context
            xnpT2 = [RE(nc.sbuf_tensor(f"xnpT{i}", [128, KC, 512], BF16))
                     for i in range(2)]
            ktok = RE(nc.sbuf_tensor("ktok", [128, 4, D], BF16))
            vtok = RE(nc.sbuf_tensor("vtok", [128, 4, D], BF16))
            wsl = [RE(nc.sbuf_tensor(f"rwsl{i}", [128, KC, 256], BF16))
                   for i in range(2)]
            xbuf = [RE(nc.sbuf_tensor(f"rxb{i}", [128, D], F32)) for i in range(2)]
            xn_tok = [RE(nc.sbuf_tensor(f"rxn{i}", [128, D], BF16)) for i in range(2)]
            dump = RE(nc.sbuf_tensor("rdump", [128, D], BF16))
            lnst = [RE(nc.sbuf_tensor(f"rlnst{i}", [128, 8], F32)) for i in range(2)]
            ksc = [RE(nc.sbuf_tensor(f"rksc{i}", [128, 128], BF16)) for i in range(2)]
            bufs = (xbuf, xn_tok, dump, lnst)

            for q in range(4):
                xnpT, xnpn = xnpT2[q % 2], f"xnpT{q % 2}"
                emit_ln1(x_peer_v, q * 4, (q + 1) * 4, xnpT, xnpn, bufs)
                for wsrc, dstt, dsn in ((wk_v, ktok, "ktok"), (wv_v, vtok, "vtok")):
                    for es in range(8):
                        wb, wbn = wsl[es % 2], f"rwsl{es % 2}"
                        dma("gpdma", wb[:, :, :], wsrc[es], wr=[wbn])
                        for tc4 in range(4):
                            b = tc4 % 2
                            def mm_fn(t, tc4=tc4, wb=wb, b=b):
                                for kc in range(KC):
                                    inst = t.matmul(
                                        bank[b][:, 0:256],
                                        xnpT[:, kc, tc4 * 128:(tc4 + 1) * 128],
                                        wb[:, kc, :],
                                        start=(kc == 0), stop=(kc == KC - 1))
                                return inst
                            P.op("pe", mm_fn, rd=[xnpn, wbn], wr=[BN[b]])
                            P.op("se", lambda s, tc4=tc4, es=es, dstt=dstt, b=b:
                                 s.activation(dstt[:, tc4, es * 256:(es + 1) * 256],
                                              bank[b][:, 0:256], AF.Silu),
                                 rd=[BN[b]], wr=[f"{dsn}{tc4}"])
                for tc4 in range(4):
                    for h in range(NH):
                        kb = (tc4 * NH + h) % 2
                        P.op("ve", lambda v, tc4=tc4, h=h, kb=kb: v.tensor_scalar(
                            ksc[kb][:, :], ktok[:, tc4, h * 128:(h + 1) * 128],
                            kdb[:, h:h + 1], None, ALU.mult),
                            rd=[f"ktok{tc4}", "kdb"], wr=[f"rksc{kb}"])
                        P.op("pe", lambda t, tc4=tc4, h=h, kb=kb: t.matmul(
                            bank[2][:, 0:128], ksc[kb][:, :],
                            vtok[:, tc4, h * 128:(h + 1) * 128],
                            start=True, stop=True),
                            rd=[f"rksc{kb}", f"vtok{tc4}"], wr=["b2"])
                        P.op("ve", lambda v, h=h: v.scalar_tensor_tensor(
                            S_all[:, h * 128:(h + 1) * 128],
                            S_all[:, h * 128:(h + 1) * 128],
                            cdb[:, h:h + 1], bank[2][:, 0:128], ALU.mult, ALU.add),
                            rd=["b2", f"S{h}", "cdb"], wr=[f"S{h}"])
            P.op("ve", lambda v: v.tensor_scalar(S_all[:, :], S_all[:, :],
                                                 flag[:, :], None, ALU.mult),
                 rd=["flag"] + SN, wr=SN)
            P.flush("r_flag")

        ln_st = contextlib.ExitStack()
        LE = ln_st.enter_context
        xnT = LE(nc.sbuf_tensor("xnT", [128, KC, TOK], BF16))
        g1T = LE(nc.sbuf_tensor("g1T", [128, TOK], BF16))
        with contextlib.ExitStack() as lst:
            LE2 = lst.enter_context
            xbuf = [LE2(nc.sbuf_tensor(f"lxb{i}", [128, D], F32)) for i in range(3)]
            xn_tok = [LE2(nc.sbuf_tensor(f"lxn{i}", [128, D], BF16)) for i in range(2)]
            dump = LE2(nc.sbuf_tensor("ldump", [128, D], BF16))
            lnst = [LE2(nc.sbuf_tensor(f"llnst{i}", [128, 8], F32)) for i in range(2)]
            bufs = (xbuf, xn_tok, dump, lnst)
            emit_ln1(x_own_v, 0, NCH, xnT, "xnT", bufs)
            P.flush("l_ln1")

        for tt in range(4):
            b = tt % 2
            def mm_fn(t, tt=tt, b=b):
                for kc in range(KC):
                    inst = t.matmul(bank[b][:, 0:512], wg1_sb[:, kc, :],
                                    xnT[:, kc, tt * 512:(tt + 1) * 512],
                                    start=(kc == 0), stop=(kc == KC - 1))
                return inst
            P.op("pe", mm_fn, rd=["xnT", "wg1"], wr=[BN[b]])
            P.op("ve", lambda v, tt=tt, b=b: v.tensor_copy(
                g1T[:, tt * 512:(tt + 1) * 512], bank[b][:, 0:512]),
                rd=[BN[b]], wr=["g1T"])
        P.flush("l_g1")
        wg1_st.close()

        if DBG:
            with contextlib.ExitStack() as ds:
                stg = ds.enter_context(nc.sbuf_tensor("dbgstg", [128, TOK], F32))
                for kc in range(KC):
                    P.op("ve", lambda v, kc=kc: v.tensor_copy(stg[:, :], xnT[:, kc, :]),
                         rd=["xnT"], wr=["dstg"])
                    P.op("sydma", lambda sy, kc=kc: sy.dma_start(
                        dbgt["xnT"].ap()[:, kc * TOK:(kc + 1) * TOK], stg[:, :]),
                        rd=["dstg"], wr=["dstg"])
                P.flush("dbg_xnT")

        with contextlib.ExitStack() as gst:
            GE_ = gst.enter_context
            Qf = GE_(nc.sbuf_tensor("Qf", [128, HPG, TOK], BF16))
            Kf = GE_(nc.sbuf_tensor("Kf", [128, HPG, TOK], BF16))
            Vt = GE_(nc.sbuf_tensor("Vt", [128, NCH, HPG * 128], BF16))
            og_buf = GE_(nc.sbuf_tensor("og_buf", [128, NCH, HPG * 128], BF16))
            ms_buf = GE_(nc.sbuf_tensor("ms_buf", [128, NCH * HPG], F32))
            wgb = [GE_(nc.sbuf_tensor(f"wgb{i}", [128, KC, HPG * 128], BF16))
                   for i in range(2)]

            dma("gpdma", wgb[0][:, :, :], wv_v[0], wr=["wgb0"])
            dma("gpdma", wgb[1][:, :, :], wk_v[0], wr=["wgb1"])

            for g_i in range(NG):
                for h2 in range(HPG):
                    gh = g_i * HPG + h2
                    P.op("se", lambda s, h2=h2, gh=gh: s.activation(
                        Dt_g[:, h2 * 128:(h2 + 1) * 128], iota_fi[:, :], AF.Exp,
                        scale=ldb[:, gh:gh + 1]),
                        rd=["iota_fi", "ldb"], wr=[f"DtE{h2}"])
                    P.op("ge", lambda g, h2=h2: g.affine_select(
                        Dt_g[:, h2 * 128:(h2 + 1) * 128],
                        Dt_g[:, h2 * 128:(h2 + 1) * 128], [[1, 128]],
                        compare_op=ALU.is_ge, fill=0.0, base=0,
                        channel_multiplier=-1),
                        rd=[f"DtE{h2}"], wr=[f"Dt{h2}"])
                    P.op("se", lambda s, h2=h2, gh=gh: s.activation(
                        qd_g[:, h2 * 128:(h2 + 1) * 128], iota_f1[:, :], AF.Exp,
                        scale=ldb[:, gh:gh + 1]),
                        rd=["iota_f1", "ldb"], wr=[f"qd{h2}"])
                for tc in range(NCH):
                    b = tc % 2
                    def vmm_fn(t, tc=tc, b=b):
                        for kc in range(KC):
                            inst = t.matmul(bank[b][:, 0:256],
                                            xnT[:, kc, tc * 128:(tc + 1) * 128],
                                            wgb[0][:, kc, :],
                                            start=(kc == 0), stop=(kc == KC - 1))
                        return inst
                    P.op("pe", vmm_fn, rd=["wgb0", "xnT"], wr=[BN[b]])
                    P.op("se", lambda s, tc=tc, b=b: s.activation(
                        Vt[:, tc, :], bank[b][:, 0:256], AF.Silu),
                        rd=[BN[b]], wr=[f"Vt{tc}"])
                dma("gpdma", wgb[0][:, :, :], wq_v[g_i], wr=["wgb0"])
                for wb, wn, dst, dn in ((wgb[1], "wgb1", Kf, "Kf"),
                                        (wgb[0], "wgb0", Qf, "Qf")):
                    for h2 in range(HPG):
                        for tt in range(4):
                            b = (h2 * 4 + tt) % 2
                            def mm_fn(t, wb=wb, h2=h2, tt=tt, b=b):
                                for kc in range(KC):
                                    inst = t.matmul(
                                        bank[b][:, 0:512],
                                        wb[:, kc, h2 * 128:(h2 + 1) * 128],
                                        xnT[:, kc, tt * 512:(tt + 1) * 512],
                                        start=(kc == 0), stop=(kc == KC - 1))
                                return inst
                            P.op("pe", mm_fn, rd=[wn, "xnT"], wr=[BN[b]])
                            P.op("se", lambda s, dst=dst, h2=h2, tt=tt, b=b:
                                 s.activation(dst[:, h2, tt * 512:(tt + 1) * 512],
                                              bank[b][:, 0:512], AF.Silu),
                                 rd=[BN[b]], wr=[dn])
                if g_i + 1 < NG:
                    dma("gpdma", wgb[0][:, :, :], wv_v[g_i + 1], wr=["wgb0"])
                    dma("gpdma", wgb[1][:, :, :], wk_v[g_i + 1], wr=["wgb1"])

                for cch in range(NCH):
                    for h2 in range(HPG):
                        gh = g_i * HPG + h2
                        rb = (cch * HPG + h2) % 2
                        P.op("pe", lambda t, h2=h2, cch=cch: t.matmul(
                            bank[2][:, 0:128], Kf[:, h2, cch * 128:(cch + 1) * 128],
                            Qf[:, h2, cch * 128:(cch + 1) * 128],
                            start=True, stop=True),
                            rd=["Kf", "Qf"], wr=["b2"])
                        P.op("pe", lambda t, h2=h2, cch=cch: t.transpose(
                            tbank[0][:, 0:128],
                            Kf[:, h2, cch * 128:(cch + 1) * 128], idn_b[:, :]),
                            rd=["Kf", "idn_b"], wr=["tb0"])
                        P.op("ve", lambda v, h2=h2, rb=rb: v.tensor_tensor(
                            sdt_b[rb][:, :], bank[2][:, 0:128],
                            Dt_g[:, h2 * 128:(h2 + 1) * 128], ALU.mult),
                            rd=["b2", f"Dt{h2}"], wr=[f"sdt{rb}"])
                        P.op("ve", lambda v, gh=gh, rb=rb: v.tensor_scalar(
                            kscb[rb][:, :], tbank[0][:, 0:128], kdb[:, gh:gh + 1],
                            None, ALU.mult), rd=["tb0", "kdb"], wr=[f"ksc{rb}"])
                        P.op("ve", lambda v, h2=h2, cch=cch, rb=rb: v.tensor_tensor(
                            qs_b[rb][:, :], Qf[:, h2, cch * 128:(cch + 1) * 128],
                            qd_g[:, h2 * 128:(h2 + 1) * 128], ALU.mult),
                            rd=["Qf", f"qd{h2}"], wr=[f"qs{rb}"])
                        P.op("se", lambda s, gh=gh, rb=rb: s.copy(
                            sbf_b[rb][:, :], S_all[:, gh * 128:(gh + 1) * 128]),
                            rd=[f"S{gh}"], wr=[f"sbf{rb}"])
                        P.op("pe", lambda t, cch=cch, h2=h2, rb=rb: t.matmul(
                            bank[3][:, 0:128], kscb[rb][:, :],
                            Vt[:, cch, h2 * 128:(h2 + 1) * 128],
                            start=True, stop=True),
                            rd=[f"ksc{rb}", f"Vt{cch}"], wr=["b3"])
                        def o_fn(t, cch=cch, h2=h2, rb=rb):
                            t.matmul(bank[0][:, 0:128], sdt_b[rb][:, :],
                                     Vt[:, cch, h2 * 128:(h2 + 1) * 128],
                                     start=True, stop=False)
                            return t.matmul(bank[0][:, 0:128], qs_b[rb][:, :],
                                            sbf_b[rb][:, :], start=False, stop=True)
                        P.op("pe", o_fn,
                             rd=[f"sdt{rb}", f"Vt{cch}", f"qs{rb}", f"sbf{rb}"],
                             wr=["b0"])
                        P.op("pe", lambda t, cch=cch, gh=gh: t.matmul(
                            bank[1][:, 0:128], g1T[:, cch * 128:(cch + 1) * 128],
                            wg2_sb[:, gh * 128:(gh + 1) * 128],
                            start=True, stop=True),
                            rd=["g1T", "wg2"], wr=["b1"])
                        P.op("ve", lambda v, gh=gh: v.scalar_tensor_tensor(
                            S_all[:, gh * 128:(gh + 1) * 128],
                            S_all[:, gh * 128:(gh + 1) * 128],
                            cdb[:, gh:gh + 1], bank[3][:, 0:128],
                            ALU.mult, ALU.add),
                            rd=["b3", f"S{gh}", "cdb"], wr=[f"S{gh}"])
                        P.op("se", lambda s, rb=rb: s.activation(
                            sg_b[rb][:, :], bank[1][:, 0:128], AF.Sigmoid),
                            rd=["b1"], wr=[f"sg{rb}"])
                        P.op("ve", lambda v, cch=cch, h2=h2, rb=rb: v.tensor_tensor(
                            og_buf[:, cch, h2 * 128:(h2 + 1) * 128],
                            bank[0][:, 0:128], sg_b[rb][:, :], ALU.mult),
                            rd=["b0", f"sg{rb}"], wr=[f"og{cch}"])
                        P.op("se", lambda s, cch=cch, h2=h2: s.activation(
                            dmp[:, :], og_buf[:, cch, h2 * 128:(h2 + 1) * 128],
                            AF.Square,
                            accum_out=ms_buf[:, cch * HPG + h2:cch * HPG + h2 + 1]),
                            rd=[f"og{cch}"], wr=[f"ms{cch}", "dmp"])

                for cch in range(NCH):
                    h8 = cch // NHC
                    dsto = onorm[h8]
                    col = (cch % NHC) * 128
                    P.op("ve", lambda v, cch=cch: v.tensor_scalar(
                        sdsc[:, 0:2], ms_buf[:, cch * HPG:(cch + 1) * HPG],
                        1.0 / 128.0, EPS, ALU.mult, ALU.add),
                        rd=[f"ms{cch}"], wr=["sdscE"])
                    P.op("se", lambda s: s.activation(
                        sdsc[:, 0:2], sdsc[:, 0:2], AF.Sqrt),
                        rd=["sdscE"], wr=["sdsc"])
                    P.op("ve", lambda v: v.reciprocal(rstd_b[:, 0:2], sdsc[:, 0:2]),
                         rd=["sdsc"], wr=["rstdb"])
                    for h2 in range(HPG):
                        gh = g_i * HPG + h2
                        rb = (cch * HPG + h2) % 2
                        P.op("ve", lambda v, cch=cch, h2=h2, rb=rb: v.tensor_scalar(
                            on_b[rb][:, :], og_buf[:, cch, h2 * 128:(h2 + 1) * 128],
                            rstd_b[:, h2:h2 + 1], None, ALU.mult),
                            rd=[f"og{cch}", "rstdb"], wr=[f"on{rb}"])
                        P.op("pe", lambda t, rb=rb: t.transpose(
                            tbank[1][:, rb * 128:(rb + 1) * 128], on_b[rb][:, :],
                            idn_b[:, :]), rd=[f"on{rb}", "idn_b"], wr=["tb1"])
                        P.op("ve", lambda v, rb=rb, gh=gh, dsto=dsto, col=col, h8=h8:
                             v.tensor_scalar(dsto[:, gh, col:col + 128],
                                             tbank[1][:, rb * 128:(rb + 1) * 128],
                                             normw[:, gh:gh + 1], None, ALU.mult),
                             rd=["tb1", "normw"], wr=[f"on{h8}_{gh}"])
                if g_i == NG - 1:
                    P.flush("groups")

            if DBG:
                P.op("sydma", lambda sy: sy.dma_start(dbgt["S"].ap(), S_all[:, :]),
                     rd=SN, wr=["dbgsdma"])
                with contextlib.ExitStack() as ds:
                    stq = ds.enter_context(nc.sbuf_tensor("dbgstq", [128, 1024], F32))
                    for h2 in range(HPG):
                        for hv in range(2):
                            P.op("ve", lambda v, h2=h2, hv=hv: v.tensor_copy(
                                stq[:, :], Qf[:, h2, hv * 1024:(hv + 1) * 1024]),
                                rd=["Qf"], wr=["dstq"])
                            P.op("sydma", lambda sy, h2=h2, hv=hv: sy.dma_start(
                                dbgt["qf"].ap()[:, h2 * TOK + hv * 1024:
                                                h2 * TOK + (hv + 1) * 1024],
                                stq[:, :]), rd=["dstq"], wr=["dstq"])
                            P.op("ve", lambda v, h2=h2, hv=hv: v.tensor_copy(
                                stq[:, :], Kf[:, h2, hv * 1024:(hv + 1) * 1024]),
                                rd=["Kf"], wr=["dstq"])
                            P.op("sydma", lambda sy, h2=h2, hv=hv: sy.dma_start(
                                dbgt["kf"].ap()[:, h2 * TOK + hv * 1024:
                                                h2 * TOK + (hv + 1) * 1024],
                                stq[:, :]), rd=["dstq"], wr=["dstq"])
                    P.flush("dbg_qf")
                with contextlib.ExitStack() as ds:
                    stg = ds.enter_context(nc.sbuf_tensor("dbgstg2", [128, 256], F32))
                    for cch in range(NCH):
                        P.op("ve", lambda v, cch=cch: v.tensor_copy(
                            stg[:, cch * 128:(cch + 1) * 128]
                            if False else stg[:, 0:256], og_buf[:, cch, :]),
                            rd=[f"og{cch}"], wr=["dstg2"])
                        P.op("sydma", lambda sy, cch=cch: sy.dma_start(
                            dbgt["og"].ap()[:, cch * 256:(cch + 1) * 256],
                            stg[:, 0:256]), rd=["dstg2"], wr=["dstg2"])
                    P.flush("dbg_og")

        ln_st.close()
        at_st.close()

        for hf in range(2):
            onrm = onorm[hf]
            onn = [f"on{hf}_{gh}" for gh in range(NH)]
            YN = [f"y{hf}_{o2}" for o2 in range(KC)]
            with contextlib.ExitStack() as hst:
                HE = hst.enter_context
                yacc = HE(nc.sbuf_tensor(f"yacc{hf}", [128, KC, HALF], F32))
                xn2T = HE(nc.sbuf_tensor(f"xn2T{hf}", [128, KC, HALF], BF16))

                with contextlib.ExitStack() as wst:
                    WE = wst.enter_context
                    xbuf = [WE(nc.sbuf_tensor(f"h{hf}xb{i}", [128, D], F32))
                            for i in range(2)]
                    wog = WE(nc.sbuf_tensor(f"wog{hf}", [128, KC, 256], BF16))
                    for tc8 in range(NHC):
                        tc = hf * NHC + tc8
                        xb, xbn = xbuf[tc8 % 2], f"hxb{tc8 % 2}"
                        dma("sydma", xb[:, :], x_own_v[tc], wr=[xbn])
                        for q in range(4):
                            b = 4 + (q % 2)
                            def tr_fn(t, xb=xb, q=q, b=b):
                                for k4 in range(4):
                                    kc = q * 4 + k4
                                    inst = t.transpose(
                                        bank[b][:, k4 * 128:(k4 + 1) * 128],
                                        xb[:, kc * 128:(kc + 1) * 128], idn_f[:, :])
                                return inst
                            P.op("pe", tr_fn, rd=[xbn, "idn_f"], wr=[BN[b]])
                            def ev_fn(v, tc8=tc8, q=q, b=b):
                                for k4 in range(4):
                                    kc = q * 4 + k4
                                    inst = v.tensor_copy(
                                        yacc[:, kc, tc8 * 128:(tc8 + 1) * 128],
                                        bank[b][:, k4 * 128:(k4 + 1) * 128])
                                return inst
                            P.op("ve", ev_fn, rd=[BN[b]],
                                 wr=[YN[q * 4 + k4] for k4 in range(4)])

                    for sl in range(8):
                        dma("gpdma", wog[:, :, :], wo_v[sl], wr=["wog"])
                        for o2h in range(2):
                            o2 = sl * 2 + o2h
                            for tt in range(2):
                                b = (o2h * 2 + tt) % 2
                                def mm_fn(t, o2h=o2h, tt=tt, b=b):
                                    for kc in range(KC):
                                        inst = t.matmul(
                                            bank[b][:, 0:512],
                                            wog[:, kc, o2h * 128:(o2h + 1) * 128],
                                            onrm[:, kc, tt * 512:(tt + 1) * 512],
                                            start=(kc == 0), stop=(kc == KC - 1))
                                    return inst
                                P.op("pe", mm_fn, rd=["wog"] + onn, wr=[BN[b]])
                                P.op("ve", lambda v, o2=o2, tt=tt, b=b:
                                     v.tensor_tensor(
                                         yacc[:, o2, tt * 512:(tt + 1) * 512],
                                         yacc[:, o2, tt * 512:(tt + 1) * 512],
                                         bank[b][:, 0:512], ALU.add),
                                     rd=[BN[b], YN[o2]], wr=[YN[o2]])
                    P.flush(f"h{hf}_wo")

                if hf == 0:
                    on_st0.close()
                else:
                    on_st1.close()

                if DBG and hf == 0:
                    with contextlib.ExitStack() as ds:
                        stg = ds.enter_context(
                            nc.sbuf_tensor("dbgstg3", [128, HALF], F32))
                        for kc in range(KC):
                            P.op("ve", lambda v, kc=kc: v.tensor_copy(
                                stg[:, :], yacc[:, kc, :]), rd=[YN[kc]], wr=["ds3"])
                            P.op("sydma", lambda sy, kc=kc: sy.dma_start(
                                dbgt["yA"].ap()[:, kc * HALF:(kc + 1) * HALF],
                                stg[:, :]), rd=["ds3"], wr=["ds3"])
                        P.flush("dbg_yA")

                with contextlib.ExitStack() as lst2:
                    L2 = lst2.enter_context
                    ybf = [L2(nc.sbuf_tensor(f"h{hf}ybf{i}", [128, 512], BF16))
                           for i in range(2)]
                    r_m = L2(nc.sbuf_tensor(f"r_m{hf}", [1, 512], F32))
                    r_ms = L2(nc.sbuf_tensor(f"r_ms{hf}", [1, 512], F32))
                    r_t = L2(nc.sbuf_tensor(f"r_t{hf}", [1, 512], F32))
                    r_b = L2(nc.sbuf_tensor(f"r_b{hf}", [1, 512], F32))
                    ab = L2(nc.sbuf_tensor(f"ab{hf}", [128, 512], F32))
                    bb = L2(nc.sbuf_tensor(f"bb{hf}", [128, 512], F32))
                    t0b = L2(nc.sbuf_tensor(f"t0b{hf}", [128, 512], F32))
                    probs = L2(nc.sbuf_tensor(f"probs{hf}", [128, 16], F32))
                    ptmp = L2(nc.sbuf_tensor(f"ptmp{hf}", [128, 16], F32))
                    peq = L2(nc.sbuf_tensor(f"peq{hf}", [128, 16], F32))
                    pst = L2(nc.sbuf_tensor(f"pst{hf}", [128, 4], F32))
                    combT = L2(nc.sbuf_tensor(f"combT{hf}", [16, HALF], F32))

                    for tt in range(2):
                        for kc in range(KC):
                            rb = kc % 2
                            P.op("se", lambda s, kc=kc, tt=tt, rb=rb: s.copy(
                                ybf[rb][:, :], yacc[:, kc, tt * 512:(tt + 1) * 512]),
                                rd=[YN[kc]], wr=[f"ybf{rb}"])
                            P.op("pe", lambda t, kc=kc, rb=rb: t.matmul(
                                bank[5][0:1, 0:512], ones_b[:, 0:1], ybf[rb][:, :],
                                start=(kc == 0), stop=(kc == KC - 1)),
                                rd=[f"ybf{rb}", "ones_b"], wr=["b5"])
                        for kc in range(KC):
                            rb = kc % 2
                            P.op("se", lambda s, kc=kc, tt=tt, rb=rb: s.activation(
                                ybf[rb][:, :], yacc[:, kc, tt * 512:(tt + 1) * 512],
                                AF.Square), rd=[YN[kc]], wr=[f"ybf{rb}"])
                            P.op("pe", lambda t, kc=kc, rb=rb: t.matmul(
                                bank[4][0:1, 0:512], ones_b[:, 0:1], ybf[rb][:, :],
                                start=(kc == 0), stop=(kc == KC - 1)),
                                rd=[f"ybf{rb}", "ones_b"], wr=["b4"])
                        P.op("ve", lambda v: v.tensor_scalar(
                            r_m[0:1, :], bank[5][0:1, 0:512], 1.0 / D, None,
                            ALU.mult), rd=["b5"], wr=["r_m"])
                        P.op("ve", lambda v: v.tensor_scalar(
                            r_ms[0:1, :], bank[4][0:1, 0:512], 1.0 / D, None,
                            ALU.mult), rd=["b4"], wr=["r_ms"])
                        P.op("ve", lambda v: v.tensor_tensor(
                            r_t[0:1, :], r_m[0:1, :], r_m[0:1, :], ALU.mult),
                            rd=["r_m"], wr=["r_t"])
                        P.op("ve", lambda v: v.tensor_tensor(
                            r_ms[0:1, :], r_ms[0:1, :], r_t[0:1, :], ALU.subtract),
                            rd=["r_ms", "r_t"], wr=["r_ms"])
                        P.op("ve", lambda v: v.tensor_scalar(
                            r_ms[0:1, :], r_ms[0:1, :], EPS, None, ALU.add),
                            rd=["r_ms"], wr=["r_ms"])
                        P.op("se", lambda s: s.activation(
                            r_t[0:1, :], r_ms[0:1, :], AF.Sqrt),
                            rd=["r_ms"], wr=["r_t"])
                        P.op("ve", lambda v: v.reciprocal(r_ms[0:1, :], r_t[0:1, :]),
                             rd=["r_t"], wr=["r_ms"])
                        P.op("ve", lambda v: v.tensor_scalar(
                            r_t[0:1, :], r_ms[0:1, :], -1.0, None, ALU.mult),
                            rd=["r_ms"], wr=["r_t"])
                        P.op("ve", lambda v: v.tensor_tensor(
                            r_b[0:1, :], r_m[0:1, :], r_t[0:1, :], ALU.mult),
                            rd=["r_m", "r_t"], wr=["r_b"])
                        P.op("pe", lambda t: t.matmul(
                            bank[4][:, 0:512], ones[0:1, :], r_ms[0:1, :],
                            start=True, stop=True),
                            rd=["r_ms", "ones"], wr=["b4"])
                        P.op("ve", lambda v: v.tensor_copy(ab[:, :],
                                                           bank[4][:, 0:512]),
                             rd=["b4"], wr=["abf"])
                        P.op("pe", lambda t: t.matmul(
                            bank[4][:, 0:512], ones[0:1, :], r_b[0:1, :],
                            start=True, stop=True),
                            rd=["r_b", "ones"], wr=["b4"])
                        P.op("ve", lambda v: v.tensor_copy(bb[:, :],
                                                           bank[4][:, 0:512]),
                             rd=["b4"], wr=["bbf"])
                        for kc in range(KC):
                            P.op("ve", lambda v, kc=kc, tt=tt: v.tensor_tensor(
                                t0b[:, :], yacc[:, kc, tt * 512:(tt + 1) * 512],
                                ab[:, :], ALU.mult),
                                rd=[YN[kc], "abf"], wr=["t0b"])
                            P.op("ve", lambda v, kc=kc, tt=tt: v.tensor_tensor(
                                xn2T[:, kc, tt * 512:(tt + 1) * 512], t0b[:, :],
                                bb[:, :], ALU.add),
                                rd=["t0b", "bbf"], wr=["xn2T"])

                    if DBG and hf == 0:
                        with contextlib.ExitStack() as ds:
                            stg = ds.enter_context(
                                nc.sbuf_tensor("dbgstg4", [128, HALF], F32))
                            for kc in range(KC):
                                P.op("ve", lambda v, kc=kc: v.tensor_copy(
                                    stg[:, :], xn2T[:, kc, :]),
                                    rd=["xn2T"], wr=["ds4"])
                                P.op("sydma", lambda sy, kc=kc: sy.dma_start(
                                    dbgt["xn2A"].ap()[:, kc * HALF:(kc + 1) * HALF],
                                    stg[:, :]), rd=["ds4"], wr=["ds4"])
                            P.flush("dbg_xn2A")

                    for tc in range(NHC):
                        def lg_fn(t, tc=tc):
                            for kc in range(KC):
                                inst = t.matmul(bank[5][:, 0:16],
                                                xn2T[:, kc, tc * 128:(tc + 1) * 128],
                                                gatewT[:, kc, :],
                                                start=(kc == 0), stop=(kc == KC - 1))
                            return inst
                        P.op("pe", lg_fn, rd=["xn2T", "gatewT"], wr=["b5"])
                        P.op("ve", lambda v: v.reduce_max(
                            pst[:, 0:1], bank[5][:, 0:16], axis=AX.X, negate=True),
                            rd=["b5"], wr=["pst"])
                        P.op("se", lambda s: s.activation(
                            probs[:, :], bank[5][:, 0:16], AF.Exp,
                            bias=pst[:, 0:1], accum_out=pst[:, 1:2]),
                            rd=["b5", "pst"], wr=["probs", "pst"])
                        P.op("ve", lambda v: v.reciprocal(pst[:, 2:3], pst[:, 1:2]),
                             rd=["pst"], wr=["pst"])
                        P.op("ve", lambda v: v.tensor_scalar(
                            probs[:, :], probs[:, :], pst[:, 2:3], None, ALU.mult),
                            rd=["probs", "pst"], wr=["probs"])
                        P.op("ve", lambda v: v.tensor_copy(ptmp[:, :], probs[:, :]),
                             rd=["probs"], wr=["ptmp"])
                        for r in range(5):
                            P.op("ve", lambda v: v.reduce_max(
                                pst[:, 3:4], ptmp[:, :], axis=AX.X),
                                rd=["ptmp"], wr=["pst"])
                            P.op("ve", lambda v: v.tensor_scalar(
                                peq[:, :], ptmp[:, :], pst[:, 3:4], None, ALU.is_ge),
                                rd=["ptmp", "pst"], wr=["peq"])
                            P.op("ve", lambda v: v.scalar_tensor_tensor(
                                ptmp[:, :], peq[:, :], -1e9, ptmp[:, :],
                                ALU.mult, ALU.add),
                                rd=["peq", "ptmp"], wr=["ptmp"])
                        P.op("ve", lambda v: v.reduce_max(
                            pst[:, 3:4], ptmp[:, :], axis=AX.X),
                            rd=["ptmp"], wr=["pst"])
                        P.op("ve", lambda v: v.tensor_scalar(
                            peq[:, :], probs[:, :], pst[:, 3:4], None, ALU.is_ge),
                            rd=["probs", "pst"], wr=["peq"])
                        P.op("ve", lambda v: v.tensor_tensor(
                            probs[:, :], probs[:, :], peq[:, :], ALU.mult),
                            rd=["probs", "peq"], wr=["probs"])
                        P.op("ve", lambda v: v.reduce_sum(
                            pst[:, 3:4], probs[:, :], axis=AX.X),
                            rd=["probs"], wr=["pst"])
                        P.op("ve", lambda v: v.tensor_scalar(
                            pst[:, 3:4], pst[:, 3:4], 1e-20, None, ALU.add),
                            rd=["pst"], wr=["pst"])
                        P.op("ve", lambda v: v.reciprocal(pst[:, 2:3], pst[:, 3:4]),
                             rd=["pst"], wr=["pst"])
                        P.op("ve", lambda v: v.tensor_scalar(
                            probs[:, :], probs[:, :], pst[:, 2:3], None, ALU.mult),
                            rd=["probs", "pst"], wr=["probs"])
                        P.op("pe", lambda t: t.transpose(
                            bank[4][0:16, 0:128], probs[:, :], idn_f[:, :]),
                            rd=["probs", "idn_f"], wr=["b4"])
                        P.op("ve", lambda v, tc=tc: v.tensor_copy(
                            combT[:, tc * 128:(tc + 1) * 128], bank[4][0:16, 0:128]),
                            rd=["b4"], wr=["combT"])
                    dma("sydma", comb_dram.ap(), combT[:, :], rd=["combT"],
                        wr=["combdram"])
                    if DBG and hf == 0:
                        P.op("sydma", lambda sy: sy.dma_start(
                            dbgt["cA"].ap(), combT[:, :]), rd=["combT"],
                            wr=["dbgcdma"])
                    P.flush(f"h{hf}_gate")

                with contextlib.ExitStack() as fst:
                    FE = fst.enter_context
                    crow = FE(nc.sbuf_tensor(f"crow{hf}", [1, HALF], F32))
                    cbc = FE(nc.sbuf_tensor(f"cbc{hf}", [128, HALF], F32))
                    ws13 = [FE(nc.sbuf_tensor(f"h{hf}ws13_{i}", [128, KC, 256], BF16))
                            for i in range(2)]
                    ws2 = [FE(nc.sbuf_tensor(f"h{hf}ws2_{i}", [128, ICH, 256], BF16))
                           for i in range(2)]
                    hbuf = FE(nc.sbuf_tensor(f"hbuf{hf}", [128, ICH, HALF], BF16))
                    sg2 = [FE(nc.sbuf_tensor(f"h{hf}sg2_{i}", [128, 512], BF16))
                           for i in range(2)]
                    su2 = [FE(nc.sbuf_tensor(f"h{hf}su2_{i}", [128, 512], BF16))
                           for i in range(2)]
                    cd_flat = comb_dram.ap().rearrange("e t -> (e t)")

                    for e in range(NE):
                        if e < 16:
                            P.op("sydma", lambda sy, e=e: sy.dma_start(
                                crow[0:1, :], cd_flat[None, e * HALF:(e + 1) * HALF]),
                                rd=["combdram"], wr=["crow"])
                            for tt in range(2):
                                P.op("pe", lambda t, tt=tt: t.matmul(
                                    bank[4][:, 0:512], ones[0:1, :],
                                    crow[0:1, tt * 512:(tt + 1) * 512],
                                    start=True, stop=True),
                                    rd=["crow", "ones"], wr=["b4"])
                                P.op("ve", lambda v, tt=tt: v.tensor_copy(
                                    cbc[:, tt * 512:(tt + 1) * 512],
                                    bank[4][:, 0:512]), rd=["b4"], wr=["cbc"])
                        for j in range(ICH):
                            wb, wbn = ws13[j % 2], f"ws13_{j % 2}"
                            dma("gpdma", wb[:, :, :], w13_v[e, j], wr=[wbn])
                            for tt in range(2):
                                rb = tt
                                def g_fn(t, wb=wb, tt=tt):
                                    for kc in range(KC):
                                        inst = t.matmul(
                                            bank[0][:, 0:512], wb[:, kc, 0:128],
                                            xn2T[:, kc, tt * 512:(tt + 1) * 512],
                                            start=(kc == 0), stop=(kc == KC - 1))
                                    return inst
                                P.op("pe", g_fn, rd=[wbn, "xn2T"], wr=["b0"])
                                P.op("se", lambda s, rb=rb: s.activation(
                                    sg2[rb][:, :], bank[0][:, 0:512], AF.Silu),
                                    rd=["b0"], wr=[f"hsg{rb}"])
                                def u_fn(t, wb=wb, tt=tt):
                                    for kc in range(KC):
                                        inst = t.matmul(
                                            bank[1][:, 0:512], wb[:, kc, 128:256],
                                            xn2T[:, kc, tt * 512:(tt + 1) * 512],
                                            start=(kc == 0), stop=(kc == KC - 1))
                                    return inst
                                P.op("pe", u_fn, rd=[wbn, "xn2T"], wr=["b1"])
                                if e < 16:
                                    P.op("ve", lambda v, rb=rb, tt=tt: v.tensor_tensor(
                                        su2[rb][:, :], bank[1][:, 0:512],
                                        cbc[:, tt * 512:(tt + 1) * 512], ALU.mult),
                                        rd=["b1", "cbc"], wr=[f"hsu{rb}"])
                                else:
                                    P.op("ve", lambda v, rb=rb: v.tensor_copy(
                                        su2[rb][:, :], bank[1][:, 0:512]),
                                        rd=["b1"], wr=[f"hsu{rb}"])
                                P.op("ve", lambda v, j=j, tt=tt, rb=rb: v.tensor_tensor(
                                    hbuf[:, j, tt * 512:(tt + 1) * 512],
                                    sg2[rb][:, :], su2[rb][:, :], ALU.mult),
                                    rd=[f"hsg{rb}", f"hsu{rb}"], wr=[f"hb{j}"])
                        for sl in range(8):
                            wb, wbn = ws2[sl % 2], f"ws2_{sl % 2}"
                            dma("gpdma", wb[:, :, :], w2_v[e, sl], wr=[wbn])
                            for o2h in range(2):
                                o2 = sl * 2 + o2h
                                for tt in range(2):
                                    b = (o2h * 2 + tt) % 2
                                    def y_fn(t, wb=wb, o2h=o2h, tt=tt, b=b):
                                        for ic in range(ICH):
                                            inst = t.matmul(
                                                bank[b][:, 0:512],
                                                wb[:, ic, o2h * 128:(o2h + 1) * 128],
                                                hbuf[:, ic, tt * 512:(tt + 1) * 512],
                                                start=(ic == 0), stop=(ic == ICH - 1))
                                        return inst
                                    P.op("pe", y_fn,
                                         rd=[wbn] + [f"hb{ic}" for ic in range(ICH)],
                                         wr=[BN[b]])
                                    P.op("ve", lambda v, o2=o2, tt=tt, b=b:
                                         v.tensor_tensor(
                                             yacc[:, o2, tt * 512:(tt + 1) * 512],
                                             yacc[:, o2, tt * 512:(tt + 1) * 512],
                                             bank[b][:, 0:512], ALU.add),
                                         rd=[BN[b], YN[o2]], wr=[YN[o2]])

                    P.flush(f"h{hf}_ffn")

                with contextlib.ExitStack() as ost:
                    OE = ost.enter_context
                    outst = [OE(nc.sbuf_tensor(f"h{hf}outst{i}", [128, D], F32))
                             for i in range(2)]
                    for tc8 in range(NHC):
                        tc = hf * NHC + tc8
                        ob, obn = outst[tc8 % 2], f"outst{tc8 % 2}"
                        for q in range(4):
                            b = 4 + (q % 2)
                            def tr_fn(t, q=q, b=b, tc8=tc8):
                                for k4 in range(4):
                                    kc = q * 4 + k4
                                    inst = t.transpose(
                                        bank[b][:, k4 * 128:(k4 + 1) * 128],
                                        yacc[:, kc, tc8 * 128:(tc8 + 1) * 128],
                                        idn_f[:, :])
                                return inst
                            P.op("pe", tr_fn,
                                 rd=[YN[q * 4 + k4] for k4 in range(4)] + ["idn_f"],
                                 wr=[BN[b]])
                            def ev_fn(v, q=q, b=b, ob=ob):
                                for k4 in range(4):
                                    kc = q * 4 + k4
                                    inst = v.tensor_copy(
                                        ob[:, kc * 128:(kc + 1) * 128],
                                        bank[b][:, k4 * 128:(k4 + 1) * 128])
                                return inst
                            P.op("ve", ev_fn, rd=[BN[b]], wr=[obn])
                        dma("sydma", out_v[tc], ob[:, :], rd=[obn])
                    P.flush(f"h{hf}_out")

        fins = [(P.sems[f"{q}{i}"], P.cnt[f"{q}{i}"])
                for q in ("dw", "dx") for i in range(8)]
        with nc.Block("fin") as blk:
            @blk.sync
            def _(sy):
                for s, t in fins:
                    if t:
                        sy.wait_ge(s, t)
    return nc



def _pack_cols(w, nslice, width):
    din, dout = w.shape
    kc = din // 128
    r = w.reshape(kc, 128, nslice, width).transpose(2, 1, 0, 3)
    return np.ascontiguousarray(r.reshape(nslice, 128, kc * width))


def _layouts(inputs):
    f = np.ascontiguousarray
    x = np.asarray(inputs["L0_attn_x"], dtype=np.float32)
    ld = f(np.asarray(inputs["L0_attn_log_decay"], np.float32).reshape(1, NH))
    wq = np.asarray(inputs["L0_attn_Wq"], np.float32)
    wk = np.asarray(inputs["L0_attn_Wk"], np.float32)
    wv = np.asarray(inputs["L0_attn_Wv"], np.float32)
    wo = np.asarray(inputs["L0_attn_Wo"], np.float32)
    wg1 = np.asarray(inputs["L0_attn_Wg1"], np.float32)
    wg2 = f(np.asarray(inputs["L0_attn_Wg2"], np.float32))
    normw = f(np.asarray(inputs["L0_attn_norm_w"], np.float32).reshape(KC, 128).T)
    gatew = np.asarray(inputs["L0_ffn_gate_w"], np.float32).T
    w13 = np.asarray(inputs["L0_ffn_w13"], np.float32)
    w2 = np.asarray(inputs["L0_ffn_w2"], np.float32)
    sw13 = np.asarray(inputs["L0_ffn_sw13"], np.float32)
    sw2 = np.asarray(inputs["L0_ffn_sw2"], np.float32)

    wq8 = _pack_cols(wq, NG, 256)
    wk8 = _pack_cols(wk, NG, 256)
    wv8 = _pack_cols(wv, NG, 256)
    wo8 = _pack_cols(wo, NG, 256)
    wg1p = f(wg1.reshape(KC, 128, 128).transpose(1, 0, 2).reshape(128, KC * 128))
    gatewp = f(gatew.reshape(KC, 128, 16).transpose(1, 0, 2).reshape(128, KC * 16))

    w13T = w13.transpose(0, 2, 1)
    w13T = w13T.reshape(16, D, 2, ICH, 128).transpose(0, 1, 3, 2, 4).reshape(16, D, 2816)
    sw13T = sw13.T.reshape(D, 2, 22, 128).transpose(0, 2, 1, 3).reshape(D, 5632)
    w13all = np.concatenate([w13T, sw13T[None, :, 0:2816],
                             sw13T[None, :, 2816:5632]], axis=0)
    w13x = w13all.reshape(NE, KC, 128, ICH, 256).transpose(0, 3, 2, 1, 4)
    w13x = f(w13x.reshape(NE, ICH, 128, KC * 256))

    w2T = w2.transpose(0, 2, 1)
    sw2T = sw2.T
    w2all = np.concatenate([w2T, sw2T[None, 0:1408, :], sw2T[None, 1408:2816, :]],
                           axis=0)
    w2x = w2all.reshape(NE, ICH, 128, 8, 256).transpose(0, 3, 2, 1, 4)
    w2x = f(w2x.reshape(NE, 8, 128, ICH * 256))

    return dict(x=x, ld=ld, wq8=wq8, wk8=wk8, wv8=wv8, wo8=wo8, wg1p=wg1p,
                wg2p=wg2, normw=normw, gatewp=gatewp, w13x=w13x, w2x=w2x)


_NC_CACHE = {}
_LAST_RES = None


def kernel(**inputs):
    L = _layouts(inputs)
    x = L["x"].reshape(4, 2, TOK, D)
    flag1 = np.ones((128, 1), np.float32)
    flag0 = np.zeros((128, 1), np.float32)
    in_maps = []
    for core in range(8):
        b, hf = core // 2, core % 2
        in_maps.append(dict(
            x_own=np.ascontiguousarray(x[b, hf]),
            x_peer=np.ascontiguousarray(x[b, 0]),
            flag=(flag1 if hf == 1 else flag0),
            ld=L["ld"], wq8=L["wq8"], wk8=L["wk8"], wv8=L["wv8"], wo8=L["wo8"],
            wg1p=L["wg1p"], wg2p=L["wg2p"], normw=L["normw"], gatewp=L["gatewp"],
            w13x=L["w13x"], w2x=L["w2x"]))
    if "nc" not in _NC_CACHE:
        _NC_CACHE["nc"] = build_nc()
    nc = _NC_CACHE["nc"]
    res = run_bass_kernel_spmd(nc, in_maps, core_ids=list(range(8)))
    global _LAST_RES
    _LAST_RES = res
    out = np.empty((4, 2, TOK, D), np.float32)
    for core in range(8):
        b, hf = core // 2, core % 2
        out[b, hf] = res.results[core]["out"]
    return out.reshape(4, 4096, D)
